# version 17
# baseline (speedup 1.0000x reference)
import numpy as np
import ml_dtypes
from contextlib import ExitStack

import concourse.bass as bass
import concourse.mybir as mybir
from concourse.bass_utils import run_bass_kernel_spmd

F32 = mybir.dt.float32
BF16 = mybir.dt.bfloat16
U32 = mybir.dt.uint32
AF = mybir.ActivationFunctionType
ALU = mybir.AluOpType
AX = mybir.AxisListType

T = 2048
D = 1024
NT = T // 128
KC = D // 128
NPAIR = 8
WIN = 512
FF = 4096
NE = 8


class Sem:
    def __init__(self, es, nc, name):
        self.h = es.enter_context(nc.semaphore(name))
        self.n = 0

    def inc(self, inst, k=1):
        self.n += k
        inst.then_inc(self.h, k)
        return self.n


def build_nc(debug=0):
    nc = bass.Bass("TRN2", target_bir_lowering=False,
                   detect_race_conditions=True)
    es = ExitStack()

    x_d = nc.dram_tensor("x", [T, D], F32, kind="ExternalInput")
    wqkvT_d = nc.dram_tensor("wqkvT", [D, 3 * D], F32, kind="ExternalInput")
    bqkv_d = nc.dram_tensor("bqkv", [3 * D], F32, kind="ExternalInput")
    woutT_d = nc.dram_tensor("woutT", [D, D], F32, kind="ExternalInput")
    bout_d = nc.dram_tensor("bout", [D], F32, kind="ExternalInput")
    wr_d = nc.dram_tensor("wr", [D, NE], F32, kind="ExternalInput")
    w1_d = nc.dram_tensor("w1", [NE, D, FF], F32, kind="ExternalInput")
    w2_d = nc.dram_tensor("w2", [NE, FF, D], F32, kind="ExternalInput")
    y_d = nc.dram_tensor("y", [T + 8, D], F32, kind="ExternalOutput")
    CAP = 640
    NG = CAP // 128
    xnb_d = nc.dram_tensor("xnb", [T + 8, D], BF16, kind="Internal")
    idxg_d = nc.dram_tensor("idxg", [NE * CAP + 128, 2], F32, kind="Internal")
    w1b_d = nc.dram_tensor("w1b", [NE, D, FF], BF16, kind="Internal")
    w2b_d = nc.dram_tensor("w2b", [NE, FF, D], BF16, kind="Internal")
    wqkvb_d = nc.dram_tensor("wqkvb", [D, 3 * D], BF16, kind="Internal")
    woutb_d = nc.dram_tensor("woutb", [D, D], BF16, kind="Internal")
    ident_c = nc.dram_tensor("ident_c", [128, 128], BF16, kind="ExternalInput")
    ones_c = nc.dram_tensor("ones_c", [1, 128], BF16, kind="ExternalInput")
    iota8_c = nc.dram_tensor("iota8_c", [128, 8], F32, kind="ExternalInput")
    bandn_c = nc.dram_tensor("bandn_c", [128, 640], BF16, kind="ExternalInput")
    tri_c = nc.dram_tensor("tri_c", [128, 128], BF16, kind="ExternalInput")
    ones128_c = nc.dram_tensor("ones128_c", [128, 128], BF16, kind="ExternalInput")
    tokid_c = nc.dram_tensor("tokid_c", [128, NT], F32, kind="ExternalInput")
    ident32_c = nc.dram_tensor("ident32_c", [128, 128], F32, kind="ExternalInput")

    SB = nc.alloc_sbuf_tensor
    y_sb = SB("y_sb", [128, NT * D], F32)
    xn_fm = SB("xn_fm", [128, KC * T], BF16)
    ident = SB("ident", [128, 128], BF16)
    ones_r = SB("ones_r", [1, 128], BF16)
    iota8 = SB("iota8", [128, 8], F32)
    bandn = SB("bandn", [128, 640], BF16)
    bq_sb = SB("bq_sb", [128, 24], F32)
    bq8_sb = SB("bq8_sb", [128, 8], F32)
    bout_r = SB("bout_r", [1, D], BF16)
    wr_sb = SB("wr_sb", [128, KC * NE], F32)
    ident32 = SB("ident32", [128, 128], F32)
    logit_sb = SB("logit_sb", [128, NT * 8], F32)
    zrow = SB("zrow", [1, D], BF16)
    gsel = SB("gsel", [128, 2 * NT], F32)
    probs = SB("probs", [128, NT * 8], F32)
    gates = SB("gates", [128, NT * 8], F32)
    s_mean = SB("s_mean", [128, NT], F32)
    s_aux = SB("s_aux", [128, 24], F32)
    s_rstd = SB("s_rstd", [128, NT], F32)
    s_ssq = SB("s_ssq", [128, NT], F32)
    s_m2 = SB("s_m2", [128, NT], F32)
    s_nmr = SB("s_nmr", [128, NT], F32)
    s_nmx = SB("s_nmx", [128, NT], F32)
    s_sme = SB("s_sme", [128, NT], F32)
    m8a = SB("m8a", [128, NT * 8], F32)
    i8a = SB("i8a", [128, NT * 8], U32)
    i8fa = SB("i8fa", [128, NT * 8], F32)
    mska = SB("mska", [128, NT * 8], F32)
    mskb = SB("mskb", [128, NT * 8], F32)
    mskc = SB("mskc", [128, NT * 8], F32)
    tri_sb = SB("tri_sb", [128, 128], BF16)
    ones128 = SB("ones128", [128, 128], BF16)
    tokid = SB("tokid", [128, NT], F32)
    flag_bf = SB("flag_bf", [128, NT * 8], BF16)
    pos_tm = SB("pos_tm", [128, NT * 8], F32)
    possel = SB("possel", [128, 4 * NT], F32)
    scat_src = SB("scat_src", [128, NT * 4], F32)
    dest_i = SB("dest_i", [128, NT * 2], U32)
    idx_all = SB("idx_all", [128, 8 * 5 * 2], F32)
    toks_u32 = SB("toks_u32", [128, 8 * 5], U32)
    gate_sl = SB("gate_sl", [128, 8 * 5], F32)
    zfill = SB("zfill", [128, 8 * 5 * 2], F32)

    ARENA = 98304
    arena = SB("arena", [128, ARENA], mybir.dt.uint8)
    abase = nc.lookup_mloc(arena).addr

    def AT(name, shape, dtype, off):
        return nc.alloc_sbuf_tensor_at(name, shape, dtype, offset=abase + off)

    o_all = AT("o_all", [128, NPAIR * T], BF16, 0)
    q_sb = AT("q_sb", [128, T], BF16, 32768)
    k_pad = AT("k_pad", [128, T + 512], BF16, 36864)
    v_st = AT("v_st", [128, T], BF16, 41984)
    v_tm = AT("v_tm", [128, T + 512], BF16, 46080)
    attnT0 = AT("attnT0", [128, 8 * 512], BF16, 51200)
    attnT1 = AT("attnT1", [128, 8 * 512], BF16, 59392)
    p_f32 = AT("p_f32", [128, 4 * 640], BF16, 67584)
    a_bf = AT("a_bf", [128, 4 * 640], BF16, 72704)
    wq_sb = AT("wq_sb", [128, KC * 128], BF16, 77824)
    wk_sb = AT("wk_sb", [128, KC * 128], BF16, 79872)
    wv_sb = AT("wv_sb", [128, KC * 128], BF16, 81920)
    wq2_sb = AT("wq2_sb", [128, KC * 128], BF16, 92160)
    wk2_sb = AT("wk2_sb", [128, KC * 128], BF16, 94208)
    wv2_sb = AT("wv2_sb", [128, KC * 128], BF16, 96256)
    woutT_sb = AT("woutT_sb", [128, KC * 512], BF16, 83968)
    junk_sb = AT("junk_sb", [128, D], BF16, 0)
    xn_sts = [AT(f"xn_st{k}", [128, D], F32, 2048 + 4096 * k) for k in range(4)]
    xnb_sts = [AT(f"xnb_st{k}", [128, D], BF16, 18432 + 2048 * k) for k in range(4)]
    xn32b = AT("xn32b", [128, 2 * D], F32, 26624)
    h_sb = AT("h_sb", [128, 32 * 640], BF16, 0)
    xe_tm = AT("xe_tm", [128, 5 * 1024], BF16, 40960)
    xe_fm = AT("xe_fm", [128, KC * 640], BF16, 51200)
    w1_sb = AT("w1_sb", [128, 2 * KC * 512], BF16, 61440)
    w2_sb = AT("w2_sb", [128, 2 * 4 * 512], BF16, 77824)
    ysc = AT("ysc", [128, 5 * 512], F32, 86016)

    PS = nc.alloc_psum_tensor
    pb0 = PS("pb0", [128, 512], F32)
    pb1 = PS("pb1", [128, 512], F32)
    pbS0 = PS("pbS0", [128, 1024], F32)
    pbS1 = PS("pbS1", [128, 1024], F32)
    pT0 = PS("pT0", [128, 1024], BF16)
    pT1 = PS("pT1", [128, 1024], BF16)

    LD = Sem(es, nc, "LD")
    LDW = Sem(es, nc, "LDW")
    CP = Sem(es, nc, "CP")
    LDO = Sem(es, nc, "LDO")
    LDP = Sem(es, nc, "LDP")
    W1S = [Sem(es, nc, "W1A"), Sem(es, nc, "W1B")]
    W2S = [Sem(es, nc, "W2A"), Sem(es, nc, "W2B")]
    W1G = [Sem(es, nc, "W1GA"), Sem(es, nc, "W1GB")]
    W2G = [Sem(es, nc, "W2GA"), Sem(es, nc, "W2GB")]
    LX = Sem(es, nc, "LX")
    ST = Sem(es, nc, "ST")
    XNB4 = [Sem(es, nc, f"XNB{k}") for k in range(4)]
    XQ = [Sem(es, nc, f"XQ{k}") for k in range(4)]
    SCI = Sem(es, nc, "SCI")
    SCR = Sem(es, nc, "SCR")
    GX = Sem(es, nc, "GX")
    SCT = Sem(es, nc, "SCT")
    PEs = Sem(es, nc, "PEs")
    ACs = Sem(es, nc, "ACs")
    DVs = Sem(es, nc, "DVs")
    ALL_SEMS = [LD, LDW, LDO, LDP, CP, W1S[0], W1S[1], W2S[0], W2S[1],
                W1G[0], W1G[1], W2G[0], W2G[1],
                LX, ST, SCI, SCR, GX, SCT, PEs, ACs, DVs] + XNB4 + XQ

    pe, ac, dv, sy, gp = nc.tensor, nc.scalar, nc.vector, nc.sync, nc.gpsimd

    for s in ALL_SEMS:
        sy.sem_clear(s.h)
    nc.all_engine_barrier()

    bank_free = {}

    def pe_guard(name):
        if name in bank_free:
            s, v = bank_free[name]
            pe.wait_ge(s.h, v)

    LX.inc(sy.dma_start(out=ident[:], in_=ident_c[:, :]), 16)
    LX.inc(sy.dma_start(out=ones_r[0:1, :], in_=ones_c[:, :]), 16)
    LX.inc(sy.dma_start(out=iota8[:], in_=iota8_c[:, :]), 16)
    LX.inc(sy.dma_start(out=bandn[:], in_=bandn_c[:, :]), 16)
    LX.inc(sy.dma_start(out=bq_sb[:], in_=bqkv_d.ap().rearrange("(c p) -> p c", p=128),
                        allow_slow_non_contiguous=True), 16)
    LDW.inc(gp.dma_start(out=bout_r[0:1, :], in_=bout_d.ap().rearrange("(o n) -> o n", o=1)), 16)
    LX.inc(sy.dma_start(out=tri_sb[:], in_=tri_c[:, :]), 16)
    LX.inc(sy.dma_start(out=ones128[:], in_=ones128_c[:, :]), 16)
    LX.inc(sy.dma_start(out=tokid[:], in_=tokid_c[:, :]), 16)
    LX.inc(sy.dma_start(out=ident32[:], in_=ident32_c[:, :]), 16)
    LX.inc(sy.dma_start(out=wr_sb.ap().rearrange("p (c e) -> p c e", e=8),
                        in_=wr_d.ap().rearrange("(c p) e -> p c e", p=128),
                        allow_slow_non_contiguous=True), 16)
    lx_all = LX.n
    y_sb3 = y_sb.ap().rearrange("p (i d) -> p i d", d=D)
    x3 = x_d.ap().rearrange("(i p) d -> p i d", p=128)
    for q in range(4):
        eng = sy if q % 2 == 0 else ac
        XQ[q].inc(eng.dma_start(out=y_sb3[:, q * 4:(q + 1) * 4, :],
                                in_=x3[:, q * 4:(q + 1) * 4, :]), 16)

    for hp0 in (0, 1):
        dsts = (wq_sb, wk_sb, wv_sb) if hp0 == 0 else (wq2_sb, wk2_sb, wv2_sb)
        for (wsb, col0) in zip(dsts, (hp0 * 128, D + hp0 * 128, 2 * D + hp0 * 128)):
            gp.wait_ge(LDP.h, LDP.n)
            srcw = wqkvT_d[:, col0:col0 + 128].rearrange("(c p) o -> p c o", p=128)
            LDP.inc(gp.dma_start(out=wsb.ap().rearrange("p (c o) -> p c o", o=128),
                                 in_=srcw), 16)
    ld_pair01 = LDP.n
    cast_done = {}
    CP.inc(gp.dma_start(out=wqkvb_d[:, :], in_=wqkvT_d[:, :]), 16)
    cast_qkv = CP.n
    gp.wait_ge(CP.h, CP.n)
    CP.inc(gp.dma_start(out=woutb_d[:, :], in_=woutT_d[:, :]), 16)
    cast_wout = CP.n

    pe.wait_ge(LX.h, lx_all)

    ys3 = y_sb.ap().rearrange("p (i d) -> p i d", d=D)
    sq_ac = {}
    nmr_dv = {}
    gates_done = []
    ln_ac_hist = []
    trs_pv1 = []
    xn_done_dv = []
    router_pe = {}

    sq_prev = [None]

    def emit_stats(q):
        sl = slice(q * 4, q * 4 + 4)
        ac.wait_ge(XQ[q].h, 16)
        for i in range(q * 4, q * 4 + 4):
            if sq_prev[0] is not None:
                ac.wait_ge(ACs.h, sq_prev[0])
            i_sq = ac.activation(out=junk_sb[:], in_=y_sb[:, i * D:(i + 1) * D],
                                 func=AF.Square, accum_out=s_ssq[:, i:i + 1])
            sq_prev[0] = ACs.inc(i_sq)
        sq_ac[q] = sq_prev[0]
        dv.wait_ge(XQ[q].h, 16)
        dv.reduce_sum(s_mean[:, sl], ys3[:, sl, :], axis=AX.X)
        dv.drain()
        dv.tensor_scalar(out=s_mean[:, sl], in0=s_mean[:, sl], scalar1=1.0 / D,
                         scalar2=None, op0=ALU.mult)
        dv.drain()
        dv.tensor_mul(s_m2[:, sl], s_mean[:, sl], s_mean[:, sl])
        dv.drain()
        dv.tensor_scalar(out=s_m2[:, sl], in0=s_m2[:, sl], scalar1=1.0e-5,
                         scalar2=None, op0=ALU.subtract)
        dv.wait_ge(ACs.h, sq_ac[q])
        dv.drain()
        i_t = dv.scalar_tensor_tensor(out=s_rstd[:, sl], in0=s_ssq[:, sl],
                                      scalar=1.0 / D, in1=s_m2[:, sl],
                                      op0=ALU.mult, op1=ALU.subtract)
        DVs.inc(i_t)
        ac.wait_ge(DVs.h, DVs.n)
        i_r = ac.activation(out=s_rstd[:, sl], in_=s_rstd[:, sl], func=AF.Sqrt)
        ACs.inc(i_r)
        dv.wait_ge(ACs.h, ACs.n)
        dv.reciprocal(s_rstd[:, sl], s_rstd[:, sl])
        dv.drain()
        i_n = dv.scalar_tensor_tensor(out=s_nmr[:, sl], in0=s_mean[:, sl],
                                      scalar=-1.0, in1=s_rstd[:, sl],
                                      op0=ALU.mult, op1=ALU.mult)
        nmr_dv[q] = DVs.inc(i_n)

    def emit_router(i):
        base = (i % 2) * D
        pe.wait_ge(DVs.h, xn_done_dv[i])
        last = None
        for c in range(KC):
            last = pe.matmul(pbS0[:, i * 8:(i + 1) * 8],
                             xn32b[:, base + c * 128: base + (c + 1) * 128],
                             wr_sb[:, c * 8:(c + 1) * 8],
                             start=(c == 0), stop=(c == KC - 1))
        router_pe[i] = PEs.inc(last)

    def emit_tile(i):
        s4 = i % 4
        xn_st = xn_sts[s4]
        xnb_st = xnb_sts[s4]
        ys = y_sb[:, i * D:(i + 1) * D]
        if s4 == 0:
            ac.wait_ge(DVs.h, nmr_dv[i // 4])
        if i >= 4:
            ac.wait_ge(PEs.h, trs_pv1[i - 4])
            ac.wait_ge(XNB4[s4].h, XNB4[s4].n)
        i_xn = ac.activation(out=xn_st[:], in_=ys, func=AF.Identity,
                             scale=s_rstd[:, i:i + 1], bias=s_nmr[:, i:i + 1])
        i_xnb = ac.activation(out=xnb_st[:], in_=ys, func=AF.Identity,
                              scale=s_rstd[:, i:i + 1], bias=s_nmr[:, i:i + 1])
        ln_ac_hist.append(ACs.inc(i_xnb))
        sy.wait_ge(ACs.h, ln_ac_hist[i])
        XNB4[s4].inc(sy.dma_start(out=xnb_d[i * 128:(i + 1) * 128, :],
                                  in_=xnb_st[:]), 16)
        base = (i % 2) * D
        pe.wait_ge(ACs.h, ln_ac_hist[i])
        pvs = []
        for ghalf, bname in ((0, "pbS1a"), (1, "pbS1b")):
            bslice = pbS1[:, 0:512] if ghalf == 0 else pbS1[:, 512:1024]
            pe_guard(bname)
            last = None
            for cc in range(4):
                c = ghalf * 4 + cc
                last = pe.transpose(bslice[:, cc * 128:(cc + 1) * 128],
                                    xn_st[:, c * 128:(c + 1) * 128], ident32[:])
            pvs.append(PEs.inc(last))
        trs_pv1.append(PEs.n)
        if i > 0:
            emit_router(i - 1)
        for ghalf, bname in ((0, "pbS1a"), (1, "pbS1b")):
            bslice = pbS1[:, 0:512] if ghalf == 0 else pbS1[:, 512:1024]
            dv.wait_ge(PEs.h, pvs[ghalf])
            dst = bass.AP(xn_fm, ghalf * 4 * T + i * 128,
                          [[KC * T, 128], [T, 4], [1, 128]])
            dv.tensor_copy(dst, bslice.rearrange("p (c q) -> p c q", q=128))
            i_ev = dv.tensor_copy(
                xn32b[:, base + ghalf * 512: base + (ghalf + 1) * 512], bslice)
            ev = DVs.inc(i_ev)
            bank_free[bname] = (DVs, ev)
        xn_done_dv.append(DVs.n)

    emit_stats(0)
    for q in range(4):
        if q + 1 < 4:
            emit_stats(q + 1)
        for i in range(q * 4, q * 4 + 4):
            emit_tile(i)
    emit_router(NT - 1)
    xn_all_dv = max(xn_done_dv)

    dv.wait_ge(LX.h, lx_all)
    i_b = dv.tensor_scalar(out=bq8_sb[:], in0=bq_sb[:, 0:8], scalar1=0.125,
                           scalar2=None, op0=ALU.mult)
    bq8_dv = DVs.inc(i_b)
    dv.wait_ge(PEs.h, router_pe[NT - 1])
    i_lc = dv.tensor_copy(logit_sb[:], pbS0[:, 0:NT * 8])
    lc_dv = DVs.inc(i_lc)
    bank_free["pbS0a"] = (DVs, lc_dv)
    dv.drain()

    i_m = dv.reduce_max(s_nmx[:], logit_sb.ap().rearrange("p (i e) -> p i e", e=8),
                        axis=AX.X, negate=True)
    DVs.inc(i_m)
    ac.wait_ge(DVs.h, DVs.n)
    last_e = None
    for i in range(NT):
        last_e = ac.activation(out=probs[:, i * 8:(i + 1) * 8],
                               in_=logit_sb[:, i * 8:(i + 1) * 8], func=AF.Exp,
                               bias=s_nmx[:, i:i + 1],
                               accum_out=s_sme[:, i:i + 1])
    ACs.inc(last_e)
    dv.wait_ge(ACs.h, ACs.n)
    dv.reciprocal(s_sme[:], s_sme[:])
    dv.drain()
    for i in range(NT):
        dv.tensor_scalar(out=probs[:, i * 8:(i + 1) * 8],
                         in0=probs[:, i * 8:(i + 1) * 8],
                         scalar1=s_sme[:, i:i + 1], scalar2=None, op0=ALU.mult)
    dv.drain()
    for i in range(NT):
        dv.max(out=m8a[:, i * 8:(i + 1) * 8], in_=probs[:, i * 8:(i + 1) * 8])
    dv.drain()
    for i in range(NT):
        dv.max_index(i8a[:, i * 8:(i + 1) * 8], m8a[:, i * 8:(i + 1) * 8],
                     probs[:, i * 8:(i + 1) * 8])
    dv.drain()
    dv.tensor_copy(i8fa[:], i8a[:])
    dv.drain()
    for i in range(NT):
        dv.tensor_scalar(out=mska[:, i * 8:(i + 1) * 8], in0=iota8[:],
                         scalar1=i8fa[:, i * 8:i * 8 + 1], scalar2=None,
                         op0=ALU.is_equal)
        dv.tensor_scalar(out=mskb[:, i * 8:(i + 1) * 8], in0=iota8[:],
                         scalar1=i8fa[:, i * 8 + 1:i * 8 + 2], scalar2=None,
                         op0=ALU.is_equal)
    dv.drain()
    dv.tensor_add(mskc[:], mska[:], mskb[:])
    dv.drain()
    i_g = dv.tensor_mul(gates[:], mskc[:], probs[:])
    gv = DVs.inc(i_g)
    gates_done = [gv] * NT
    dv.drain()
    dv.tensor_mul(mskc[:], mska[:], probs[:])
    dv.drain()
    dv.reduce_sum(gsel.ap().rearrange("p (k i) -> p k i", k=2)[:, 0:1, :],
                  mskc.ap().rearrange("p (i e) -> p i e", e=8), axis=AX.X)
    dv.drain()
    dv.tensor_mul(mskc[:], mskb[:], probs[:])
    dv.drain()
    dv.reduce_sum(gsel.ap().rearrange("p (k i) -> p k i", k=2)[:, 1:2, :],
                  mskc.ap().rearrange("p (i e) -> p i e", e=8), axis=AX.X)
    dv.drain()
    i_f = dv.tensor_scalar(out=flag_bf[:], in0=gates[:], scalar1=0.0, scalar2=None,
                           op0=ALU.is_gt)
    flag_dv = DVs.inc(i_f)

    xn_all_dv = max(xn_done_dv)

    if debug:
        dbg_xn = nc.dram_tensor("dbg_xn", [128, KC * T], F32, kind="ExternalOutput")
        dbg_pr = nc.dram_tensor("dbg_pr", [128, NT * 8], F32, kind="ExternalOutput")
        dbg_ga = nc.dram_tensor("dbg_ga", [128, NT * 8], F32, kind="ExternalOutput")
        sy.wait_ge(DVs.h, DVs.n)
        gp.wait_ge(DVs.h, DVs.n)
        LD.inc(gp.dma_start(out=dbg_xn[:, :], in_=xn_fm[:]), 16)
        ST.inc(sy.dma_start(out=dbg_pr[:, :], in_=probs[:]), 16)
        ST.inc(sy.dma_start(out=dbg_ga[:, :], in_=gates[:]), 16)
    if debug == 1:
        ST.inc(sy.dma_start(out=y_d[0:T, :].rearrange("(i p) d -> p i d", p=128),
                            in_=y_sb.ap().rearrange("p (i d) -> p i d", d=D)), 16)
        sy.wait_ge(ST.h, ST.n)
        sy.wait_ge(LX.h, LX.n)
        gp.wait_ge(LD.h, LD.n)
        gp.wait_ge(LDW.h, LDW.n)
        gp.wait_ge(CP.h, CP.n)
        es.close()
        return nc

    from concourse.bass_core_rust import IndirectOffsetOnAxis

    def emit_compaction():
        pe.wait_ge(DVs.h, flag_dv)
        pe_guard("pb0")
        last = None
        for i in range(NT):
            for c in range(i + 1):
                last = pe.matmul(pb0[:, i * 8:(i + 1) * 8],
                                 tri_sb[:] if c == i else ones128[:],
                                 flag_bf[:, c * 8:(c + 1) * 8],
                                 start=(i == 0 and c == 0),
                                 stop=(i == NT - 1 and c == i))
        scan_pe = PEs.inc(last)
        dv.wait_ge(PEs.h, scan_pe)
        i_ev = dv.tensor_copy(pos_tm[:], pb0[:, 0:NT * 8])
        scan_dv = DVs.inc(i_ev)
        bank_free["pb0"] = (DVs, scan_dv)
        dv.drain()
        for i in range(NT):
            p8 = pos_tm[:, i * 8:(i + 1) * 8]
            dv.tensor_mul(mska[:, i * 8:(i + 1) * 8], mska[:, i * 8:(i + 1) * 8], p8)
            dv.tensor_mul(mskb[:, i * 8:(i + 1) * 8], mskb[:, i * 8:(i + 1) * 8], p8)
        dv.drain()
        pos3 = possel.ap().rearrange("p (k i) -> p k i", k=4)
        dv.reduce_sum(pos3[:, 0:1, :], mska.ap().rearrange("p (i e) -> p i e", e=8),
                      axis=AX.X, apply_transpose=False)
        dv.reduce_sum(pos3[:, 1:2, :], mskb.ap().rearrange("p (i e) -> p i e", e=8),
                      axis=AX.X, apply_transpose=False)
        dv.drain()
        ea = mskc[:, 0:2 * NT]
        i8v = i8fa.ap().rearrange("p (i e) -> p i e", e=8)
        for ch in range(2):
            dv.tensor_copy(ea[:, ch * NT:(ch + 1) * NT], i8v[:, :, ch:ch + 1])
        dv.drain()
        pos2 = possel[:, 0:2 * NT]
        dst2 = possel[:, 2 * NT:4 * NT]
        dv.scalar_tensor_tensor(out=dst2, in0=ea, scalar=float(CAP), in1=pos2,
                                op0=ALU.mult, op1=ALU.add)
        dv.drain()
        dv.tensor_scalar(out=pos2, in0=pos2, scalar1=float(CAP), scalar2=None,
                         op0=ALU.is_lt)
        dv.drain()
        dv.scalar_tensor_tensor(out=dst2, in0=dst2, scalar=1.0, in1=pos2,
                                op0=ALU.mult, op1=ALU.mult)
        dv.drain()
        dv.scalar_tensor_tensor(out=pos2, in0=pos2, scalar=-float(NE * CAP),
                                in1=dst2, op0=ALU.mult, op1=ALU.add)
        dv.drain()
        dv.tensor_scalar(out=dst2, in0=pos2, scalar1=float(NE * CAP),
                         scalar2=None, op0=ALU.add)
        for ch in range(2):
            dv.tensor_copy(bass.AP(scat_src, 2 * ch, [[64, 128], [4, NT], [1, 1]]),
                           tokid.ap().rearrange("p (i o) -> p i o", o=1))
            dv.tensor_copy(bass.AP(scat_src, 2 * ch + 1, [[64, 128], [4, NT], [1, 1]]),
                           gsel[:, ch * NT:(ch + 1) * NT].rearrange("p (i o) -> p i o", o=1))
        dv.drain()
        dv.tensor_copy(dest_i[:], possel[:, 2 * NT:4 * NT])
        dv.drain()
        zf3 = zfill.ap().rearrange("p (e g b) -> p e g b", b=2, g=NG)
        for g in range(NG):
            dv.memset(zf3[:, :, g:g + 1, 0:1], float(T + g))
        dv.memset(zf3[:, :, :, 1:2], 0.0)
        i_zr = dv.memset(zrow[0:1, :], 0.0)
        route_dv = DVs.inc(i_zr)

        gp.wait_ge(DVs.h, route_dv)
        SCI.inc(gp.dma_start(out=idxg_d[0:NE * CAP, :].rearrange("(e g p) b -> p e g b", p=128, g=NG),
                             in_=zfill.ap().rearrange("p (e g b) -> p e g b", b=2, g=NG)), 16)
        for tg in range(8):
            SCI.inc(gp.dma_start(out=xnb_d[T + tg:T + tg + 1, :], in_=zrow[0:1, :]), 16)
        gp.wait_ge(SCI.h, SCI.n)
        idxg_flat = idxg_d.ap()
        for i in range(NT):
            for ch in range(2):
                col = (i * 2 + ch) * 2
                SCI.inc(gp.indirect_dma_start(
                    out=idxg_flat,
                    out_offset=IndirectOffsetOnAxis(dest_i[:, NT * ch + i:NT * ch + i + 1], 0),
                    in_=scat_src[:, col:col + 2], in_offset=None), 16)
        gp.wait_ge(SCI.h, SCI.n)
        SCR.inc(gp.dma_start(
            out=idx_all.ap().rearrange("p (e g b) -> p e g b", b=2, g=NG),
            in_=idxg_d[0:NE * CAP, :].rearrange("(e g p) b -> p e g b", p=128, g=NG)), 16)

    def emit_route2():
        dv.wait_ge(SCR.h, SCR.n)
        ia3 = idx_all.ap().rearrange("p (s b) -> p s b", b=2)
        dv.tensor_copy(toks_u32[:], ia3[:, :, 0:1])
        i_gs = dv.tensor_copy(gate_sl[:], ia3[:, :, 1:2])
        return DVs.inc(i_gs)

    def pair_bufs(hp):
        return (wq_sb, wk_sb, wv_sb) if hp % 2 == 0 else (wq2_sb, wk2_sb, wv2_sb)

    def load_pair_w(hp):
        sy.wait_ge(CP.h, cast_qkv)
        for (wsb, col0) in zip(pair_bufs(hp), (hp * 128, D + hp * 128,
                                               2 * D + hp * 128)):
            src = wqkvb_d[:, col0:col0 + 128].rearrange("(c p) o -> p c o", p=128)
            LD.inc(sy.dma_start(out=wsb.ap().rearrange("p (c o) -> p c o", o=128),
                                in_=src), 16)
        return LD.n

    def load_woutT(dhl):
        sy.wait_ge(CP.h, cast_wout)
        src = woutb_d[:, dhl * 512:(dhl + 1) * 512] \
            .rearrange("(c p) o -> p c o", p=128)
        return LDO.inc(sy.dma_start(
            out=woutT_sb.ap().rearrange("p (c o) -> p c o", o=512), in_=src), 16)

    dv.memset(k_pad[:, 0:256], 0.0)
    dv.memset(k_pad[:, T + 256:T + 512], 0.0)
    dv.memset(v_tm[:, 0:256], 0.0)
    i_pad = dv.memset(v_tm[:, T + 256:T + 512], 0.0)
    pads_dv = DVs.inc(i_pad)

    last_av_pe = 0
    abf_slot_prev = {}
    ldw_for = {0: ld_pair01, 1: ld_pair01}

    for k in range(4):
        ac.wait_ge(XNB4[k].h, XNB4[k].n)
    for hp in range(NPAIR):
        if hp < 2:
            pe.wait_ge(LDP.h, ld_pair01)
        else:
            pe.wait_ge(LD.h, LD.n)
        if hp == 0:
            pe.wait_ge(DVs.h, xn_all_dv)
            ac.wait_ge(DVs.h, bq8_dv)
        wqp, wkp, wvp = pair_bufs(hp)
        qkv_sets = ((wvp, v_st, 16 + hp, 0), (wqp, q_sb, hp, 0),
                    (wkp, k_pad, 8 + hp, 256))
        for (wsb, dst, bidx, off) in qkv_sets:
            for tb in range(4):
                bname = f"pb{tb % 2}"
                bank = pb0 if tb % 2 == 0 else pb1
                pe_guard(bname)
                last = None
                for c in range(KC):
                    last = pe.matmul(bank[:], wsb[:, c * 128:(c + 1) * 128],
                                     xn_fm[:, c * T + tb * 512: c * T + (tb + 1) * 512],
                                     start=(c == 0), stop=(c == KC - 1))
                pv = PEs.inc(last)
                ac.wait_ge(PEs.h, pv)
                d0 = tb * 512 + off
                if wsb is wqp:
                    i_ev = ac.activation(out=dst[:, d0:d0 + 512], in_=bank[:],
                                         func=AF.Identity, scale=0.125,
                                         bias=bq8_sb[:, hp:hp + 1])
                else:
                    i_ev = ac.activation(out=dst[:, d0:d0 + 512], in_=bank[:],
                                         func=AF.Identity,
                                         bias=bq_sb[:, bidx:bidx + 1])
                av = ACs.inc(i_ev)
                bank_free[bname] = (ACs, av)
            if wsb is wvp:
                v_ac = ACs.n
        qkv_ac = ACs.n
        pe.wait_ge(ACs.h, v_ac)
        if hp == 0:
            pe.wait_ge(DVs.h, pads_dv)
        for g in range(2):
            bname = f"pT{g}"
            bankT = pT0 if g == 0 else pT1
            pe_guard(bname)
            last = None
            for j in range(8):
                jj = g * 8 + j
                last = pe.transpose(bankT[:, j * 128:(j + 1) * 128],
                                    v_st[:, jj * 128:(jj + 1) * 128], ident[:])
            pv = PEs.inc(last)
            dv.wait_ge(PEs.h, pv)
            i_ev = dv.tensor_copy(v_tm[:, 256 + g * 1024: 256 + (g + 1) * 1024],
                                  bankT[:])
            ev = DVs.inc(i_ev)
            bank_free[bname] = (DVs, ev)

        if hp + 2 < NPAIR:
            sy.wait_ge(PEs.h, PEs.n)
            ldw_for[hp + 2] = load_pair_w(hp + 2)
        elif hp == NPAIR - 2:
            sy.wait_ge(PEs.h, PEs.n)
            load_woutT(0)
        if hp == 0:
            emit_compaction()
        pe.wait_ge(ACs.h, qkv_ac)

        attn_ready = {}
        trs_done = {}
        sums_of = {}

        def aux_of(i, base, h):
            sl = (i % 2) * 2 + h
            return s_aux[:, base + sl:base + sl + 1]

        def emit_norm(i):
            dv.wait_ge(ACs.h, sums_of[i])
            i_rc = None
            for h in (0, 1):
                i_rc = dv.reciprocal(aux_of(i, 20, h), aux_of(i, 16, h))
            rc = DVs.inc(i_rc)
            ac.wait_ge(DVs.h, rc)
            par = i % 2
            if par in abf_slot_prev:
                ac.wait_ge(PEs.h, abf_slot_prev[par])
            for h in (0, 1):
                po = (par * 2 + h) * 640
                i_n = ac.activation(out=a_bf[:, po:po + 640],
                                    in_=p_f32[:, po:po + 640],
                                    func=AF.Identity, scale=aux_of(i, 20, h))
                attn_ready[(i, h)] = (ACs, ACs.inc(i_n))

        def emit_scores(i, prev):
            par = i % 2
            sc = []
            for (h, bS, row0) in ((0, pbS0, 0), (1, pbS1, 64)):
                pe_guard(f"pbS{h}a"); pe_guard(f"pbS{h}b")
                lhs = q_sb[row0:row0 + 64, i * 128:(i + 1) * 128]
                rhs0 = k_pad[row0:row0 + 64, i * 128: i * 128 + 512]
                rhs1 = k_pad[row0:row0 + 64, i * 128 + 512: i * 128 + 640]
                tp = (row0, 0)
                pe.matmul(bS[:, 0:512], lhs, rhs0, start=True, stop=False,
                          tile_position=tp)
                pe.matmul(bS[:, 512:640], lhs, rhs1, start=True, stop=False,
                          tile_position=tp)
                pe.matmul(bS[:, 0:512], ident[:], bandn[:, 0:512],
                          start=False, stop=True)
                last = pe.matmul(bS[:, 512:640], ident[:], bandn[:, 512:640],
                                 start=False, stop=True)
                sc.append(PEs.inc(last))
            rc = None
            if prev is not None:
                dv.wait_ge(ACs.h, sums_of[prev])
                i_rc = None
                for h in (0, 1):
                    i_rc = dv.reciprocal(aux_of(prev, 20, h), aux_of(prev, 16, h))
                rc = DVs.inc(i_rc)
                ppar = prev % 2
                ac.wait_ge(DVs.h, rc)
                if ppar in abf_slot_prev:
                    ac.wait_ge(PEs.h, abf_slot_prev[ppar])
                po = ppar * 2 * 640
                i_n = ac.activation(out=a_bf[:, po:po + 640],
                                    in_=p_f32[:, po:po + 640],
                                    func=AF.Identity, scale=aux_of(prev, 20, 0))
                attn_ready[(prev, 0)] = (ACs, ACs.inc(i_n))
            dv.wait_ge(PEs.h, sc[0])
            i_m = dv.reduce_max(aux_of(i, 12, 0), pbS0[:, 0:640], axis=AX.X,
                                negate=True)
            m0 = DVs.inc(i_m)
            ac.wait_ge(DVs.h, m0)
            po = (par * 2) * 640
            i_e = ac.activation(out=p_f32[:, po:po + 640], in_=pbS0[:, 0:640],
                                func=AF.Exp, bias=aux_of(i, 12, 0),
                                accum_out=aux_of(i, 16, 0))
            e0 = ACs.inc(i_e)
            bank_free["pbS0a"] = (ACs, e0)
            bank_free["pbS0b"] = (ACs, e0)
            dv.wait_ge(PEs.h, sc[1])
            i_m = dv.reduce_max(aux_of(i, 12, 1), pbS1[:, 0:640], axis=AX.X,
                                negate=True)
            m1 = DVs.inc(i_m)
            if prev is not None:
                ppar = prev % 2
                dv.wait_ge(DVs.h, rc)
                if ppar in abf_slot_prev:
                    dv.wait_ge(PEs.h, abf_slot_prev[ppar])
                po = (ppar * 2 + 1) * 640
                i_n = dv.tensor_scalar(out=a_bf[:, po:po + 640],
                                       in0=p_f32[:, po:po + 640],
                                       scalar1=aux_of(prev, 20, 1), scalar2=None,
                                       op0=ALU.mult)
                attn_ready[(prev, 1)] = (DVs, DVs.inc(i_n))
            ac.wait_ge(DVs.h, m1)
            po = (par * 2 + 1) * 640
            i_e = ac.activation(out=p_f32[:, po:po + 640], in_=pbS1[:, 0:640],
                                func=AF.Exp, bias=aux_of(i, 12, 1),
                                accum_out=aux_of(i, 16, 1))
            sums_of[i] = ACs.inc(i_e)
            bank_free["pbS1a"] = (ACs, sums_of[i])
            bank_free["pbS1b"] = (ACs, sums_of[i])

        def emit_transposes(i):
            par = i % 2
            r = i % 4
            for (h, aT) in ((0, attnT0), (1, attnT1)):
                po = (par * 2 + h) * 640
                bname = f"pT{h}"
                bankT = pT0 if h == 0 else pT1
                pe_guard(bname)
                rs, rv = attn_ready[(i, h)]
                pe.wait_ge(rs.h, rv)
                last = None
                for c in range(5):
                    last = pe.transpose(bankT[:, c * 128:(c + 1) * 128],
                                        a_bf[:, po + c * 128:po + (c + 1) * 128],
                                        ident[:])
                pv = PEs.inc(last)
                dv.wait_ge(PEs.h, pv)
                dst = bass.AP(aT, r * 640, [[8 * 512, 128], [512, 5], [1, 128]])
                i_ev = dv.tensor_copy(
                    dst, bankT[:, 0:640].rearrange("p (c q) -> p c q", q=128))
                ev = DVs.inc(i_ev)
                bank_free[bname] = (DVs, ev)
            trs_done[i] = PEs.n
            abf_slot_prev[par] = PEs.n
            return DVs.n

        def emit_av(qb, trs_dv):
            pe_guard("pb0")
            pe_guard("pb1")
            pe.wait_ge(DVs.h, trs_dv)
            last = None
            d_order = [4, 0, 1, 2, 3, 5, 6, 7]
            for d in d_order:
                j = 4 * qb + d
                q0, q1 = max(0, d - 4) * 128, (min(3, d) + 1) * 128
                for (h, aT, bk, cp) in ((0, attnT0, pb0, 0), (1, attnT1, pb1, 64)):
                    last = pe.matmul(bk[cp:cp + 64, q0:q1],
                                     v_tm[:, j * 128 + cp: j * 128 + cp + 64],
                                     aT[:, d * 512 + q0: d * 512 + q1],
                                     start=(d == 4), stop=(d == 7),
                                     tile_position=(0, cp))
            av_pe = PEs.inc(last)
            ac.wait_ge(PEs.h, av_pe)
            ocol = hp * T + qb * 512
            ac.copy(out=o_all[0:64, ocol:ocol + 512], in_=pb0[0:64, :])
            i_ev = ac.copy(out=o_all[64:128, ocol:ocol + 512], in_=pb1[64:128, :])
            av_ac = ACs.inc(i_ev)
            bank_free["pb0"] = (ACs, av_ac)
            bank_free["pb1"] = (ACs, av_ac)
            return av_pe

        order = [3, 0, 1, 2]
        seq = [qb * 4 + r for qb in range(4) for r in order]
        tdv = 0
        for k in range(len(seq)):
            if k >= 2:
                tdv = emit_transposes(seq[k - 2])
            emit_scores(seq[k], seq[k - 1] if k >= 1 else None)
            if k >= 2 and (k - 2) % 4 == 3:
                last_av_pe = attn_done_pe = emit_av((k - 2) // 4, tdv)
        emit_norm(seq[15])
        tdv = emit_transposes(seq[14])
        tdv = emit_transposes(seq[15])
        last_av_pe = attn_done_pe = emit_av(3, tdv)
        if hp == 0:
            route2_dv = emit_route2()
            for e in range(5):
                gp.wait_ge(CP.h, CP.n)
                CP.inc(gp.dma_start(out=w1b_d[e, :, :], in_=w1_d[e, :, :]), 16)
                gp.wait_ge(CP.h, CP.n)
                CP.inc(gp.dma_start(out=w2b_d[e, :, :], in_=w2_d[e, :, :]), 16)
                cast_done[e] = CP.n

    gp.wait_ge(DVs.h, route2_dv)
    for k in range(4):
        gp.wait_ge(XNB4[k].h, XNB4[k].n)
    gp.wait_ge(PEs.h, attn_done_pe)
    for g in range(5):
        GX.inc(gp.indirect_dma_start(
            out=xe_tm[:, g * 1024:(g + 1) * 1024],
            out_offset=None,
            in_=xnb_d.ap(),
            in_offset=IndirectOffsetOnAxis(toks_u32[:, g:g + 1], 0)), 16)
    gather0_done = GX.n

    oproj_ac = ACs.n
    op_banks = [("pb0", pb0[:, 0:512]), ("pb1", pb1[:, 0:512]),
                ("pbS0a", pbS0[:, 0:512]), ("pbS1a", pbS1[:, 0:512])]
    for dhl in range(2):
        pe.wait_ge(LDO.h, LDO.n)
        pe.wait_ge(LDW.h, LDW.n)
        if dhl == 0:
            pe.wait_ge(ACs.h, oproj_ac)
        for i in range(NT):
            bname, bank = op_banks[i % 4]
            pe_guard(bname)
            if bname in ("pbS0a", "pbS1a"):
                pe_guard(bname[:-1] + "b")
            for c in range(KC):
                pe.matmul(bank, o_all[:, c * T + i * 128:c * T + (i + 1) * 128],
                          woutT_sb[:, c * 512:(c + 1) * 512],
                          start=(c == 0), stop=False)
            last = pe.matmul(bank, ones_r[0:1, :],
                             bout_r[0:1, dhl * 512:(dhl + 1) * 512],
                             start=False, stop=True)
            pv = PEs.inc(last)
            dv.wait_ge(PEs.h, pv)
            ysl = y_sb[:, i * D + dhl * 512: i * D + (dhl + 1) * 512]
            i_ev = dv.scalar_tensor_tensor(out=ysl, in0=bank, scalar=1.0,
                                           in1=ysl, op0=ALU.mult, op1=ALU.add)
            ev = DVs.inc(i_ev)
            bank_free[bname] = (DVs, ev)
            if bname in ("pbS0a", "pbS1a"):
                bank_free[bname[:-1] + "b"] = (DVs, ev)
        if dhl == 0:
            sy.wait_ge(PEs.h, PEs.n)
            load_woutT(1)

    for g in range(5):
        bnameT = f"pT{g % 2}"
        bankT = pT0 if g % 2 == 0 else pT1
        pe_guard(bnameT)
        pe.wait_ge(GX.h, gather0_done)
        last = None
        for c in range(KC):
            last = pe.transpose(bankT[:, c * 128:(c + 1) * 128],
                                xe_tm[:, g * 1024 + c * 128:g * 1024 + (c + 1) * 128],
                                ident[:])
        pv = PEs.inc(last)
        dv.wait_ge(PEs.h, pv)
        dstx = bass.AP(xe_fm, g * 128, [[KC * 640, 128], [640, KC], [1, 128]])
        i_ev = dv.tensor_copy(dstx, bankT.ap().rearrange("p (c q) -> p c q", q=128))
        ev = DVs.inc(i_ev)
        bank_free[bnameT] = (DVs, ev)
    xe0_trs_pe = PEs.n
    xe0_fm_dv = DVs.n

    if debug:
        attnres_dv = DVs.n
        dbg_oa = nc.dram_tensor("dbg_oa", [128, NPAIR * T], F32, kind="ExternalOutput")
        sy.wait_ge(DVs.h, attnres_dv)
        gp.wait_ge(DVs.h, attnres_dv)
        LD.inc(gp.dma_start(out=dbg_oa[:, :], in_=o_all[:]), 16)
    if debug == 2:
        ST.inc(sy.dma_start(out=y_d[0:T, :].rearrange("(i p) d -> p i d", p=128),
                            in_=y_sb.ap().rearrange("p (i d) -> p i d", d=D)), 16)
        sy.wait_ge(ST.h, ST.n)
        sy.wait_ge(LX.h, LX.n)
        gp.wait_ge(LD.h, LD.n)
        gp.wait_ge(LDW.h, LDW.n)
        gp.wait_ge(LDO.h, LDO.n)
        es.close()
        return nc

    oproj_end_pe = PEs.n
    sy.wait_ge(DVs.h, DVs.n)
    ST.inc(sy.dma_start(out=y_d[0:T, :].rearrange("(i p) d -> p i d", p=128),
                        in_=y_sb.ap().rearrange("p (i d) -> p i d", d=D)), 16)
    base_store = ST.n

    w1buf_free = [(PEs, attn_done_pe), (PEs, attn_done_pe)]
    w2buf_free = [(PEs, oproj_end_pe), (PEs, oproj_end_pe)]
    moe_arena_pe = oproj_end_pe
    w1_loads = {}
    w2_loads = {}

    def load_w1(e, fg, buf):
        s, v = w1buf_free[buf]
        dstw = w1_sb[:, buf * KC * 512:(buf + 1) * KC * 512] \
            .rearrange("p (c f) -> p c f", f=512)
        if e in cast_done:
            sy.wait_ge(s.h, v)
            sy.wait_ge(CP.h, cast_done[e])
            srcw = w1b_d[e, :, fg * 512:(fg + 1) * 512] \
                .rearrange("(c p) f -> p c f", p=128)
            v = W1S[buf].inc(sy.dma_start(out=dstw, in_=srcw), 16)
            w1_loads[(e, fg)] = (W1S[buf], buf, v)
        else:
            gp.wait_ge(s.h, v)
            srcw = w1_d[e, :, fg * 512:(fg + 1) * 512] \
                .rearrange("(c p) f -> p c f", p=128)
            v = W1G[buf].inc(gp.dma_start(out=dstw, in_=srcw), 16)
            w1_loads[(e, fg)] = (W1G[buf], buf, v)

    def load_w2(e, dhl, fg, buf):
        s, v = w2buf_free[buf]
        dstw = w2_sb[:, buf * 4 * 512:(buf + 1) * 4 * 512] \
            .rearrange("p (c o) -> p c o", o=512)
        if e in cast_done:
            sy.wait_ge(s.h, v)
            sy.wait_ge(CP.h, cast_done[e])
            srcw = w2b_d[e, fg * 512:(fg + 1) * 512, dhl * 512:(dhl + 1) * 512] \
                .rearrange("(c p) o -> p c o", p=128)
            v = W2S[buf].inc(sy.dma_start(out=dstw, in_=srcw), 16)
            w2_loads[(e, dhl, fg)] = (W2S[buf], buf, v)
        else:
            gp.wait_ge(s.h, v)
            srcw = w2_d[e, fg * 512:(fg + 1) * 512, dhl * 512:(dhl + 1) * 512] \
                .rearrange("(c p) o -> p c o", p=128)
            v = W2G[buf].inc(gp.dma_start(out=dstw, in_=srcw), 16)
            w2_loads[(e, dhl, fg)] = (W2G[buf], buf, v)

    w1_order = [(e, fg) for e in range(NE) for fg in range(8)]
    w2_order = [(e, dhl, fg) for e in range(NE)
                for dhl in range(2) for fg in range(8)]
    w1_next = 2
    w2_next = 2
    load_w1(*w1_order[0], 0)
    load_w1(*w1_order[1], 1)
    load_w2(*w2_order[0], 0)
    load_w2(*w2_order[1], 1)

    gather_done = {}
    xe_trs_done = {}

    def gather_xe(e):
        dst = xe_tm
        gp.wait_ge(DVs.h, route2_dv)
        for k in range(4):
            gp.wait_ge(XNB4[k].h, XNB4[k].n)
        gp.wait_ge(PEs.h, xe_trs_done[e - 1])
        for g in range(NG):
            GX.inc(gp.indirect_dma_start(
                out=dst[:, g * 1024:(g + 1) * 1024],
                out_offset=None,
                in_=xnb_d.ap(),
                in_offset=IndirectOffsetOnAxis(
                    toks_u32[:, e * NG + g:e * NG + g + 1], 0)), 16)
        gather_done[e] = GX.n

    m1_banks = [("pbS0a", pbS0[:, 0:512]), ("pbS1a", pbS1[:, 0:512])]
    m2_banks = [("pb0", pb0[:, 0:512]), ("pb1", pb1[:, 0:512]),
                ("pbS0a", pbS0[:, 0:512]), ("pbS0b", pbS0[:, 512:1024]),
                ("pbS1a", pbS1[:, 0:512])]
    sct_expert_end = {}
    sct_half_prev = 0
    sct_hist = []

    def emit_xe_trans(e):
        srcb = xe_tm
        pe.wait_ge(GX.h, gather_done[e])
        for g in range(NG):
            bname = f"pT{g % 2}"
            bankT = pT0 if g % 2 == 0 else pT1
            pe_guard(bname)
            last = None
            for c in range(KC):
                last = pe.transpose(bankT[:, c * 128:(c + 1) * 128],
                                    srcb[:, g * 1024 + c * 128:g * 1024 + (c + 1) * 128],
                                    ident[:])
            pv = PEs.inc(last)
            dv.wait_ge(PEs.h, pv)
            dstx = bass.AP(xe_fm, g * 128, [[KC * 640, 128], [640, KC], [1, 128]])
            i_ev = dv.tensor_copy(dstx, bankT.ap().rearrange("p (c q) -> p c q", q=128))
            ev = DVs.inc(i_ev)
            bank_free[bname] = (DVs, ev)
        xe_trs_done[e] = PEs.n
        return DVs.n

    gather_done[0] = gather0_done
    xe_trs_done[0] = xe0_trs_pe
    xe_fm_ready = xe0_fm_dv
    gather_xe(1)

    for e in range(NE):
        first_m1 = True
        for fg in range(8):
            lsem, buf, ldv = w1_loads[(e, fg)]
            pe.wait_ge(lsem.h, ldv)
            if first_m1:
                pe.wait_ge(DVs.h, xe_fm_ready)
                first_m1 = False
            w1base = buf * KC * 512
            for fc in range(4):
                for blk, (b0, bw) in enumerate(((0, 512), (512, 128))):
                    bname, bank = m1_banks[(fc * 2 + blk) % 2]
                    pe_guard(bname)
                    pe_guard(bname[:-1] + "b")
                    last = None
                    for c in range(KC):
                        last = pe.matmul(
                            bank[:, 0:bw],
                            w1_sb[:, w1base + c * 512 + fc * 128:
                                  w1base + c * 512 + (fc + 1) * 128],
                            xe_fm[:, c * 640 + b0: c * 640 + b0 + bw],
                            start=(c == 0), stop=(c == KC - 1))
                    pv = PEs.inc(last)
                    ac.wait_ge(PEs.h, pv)
                    fi = fg * 4 + fc
                    i_ev = ac.activation(
                        out=h_sb[:, fi * 640 + b0: fi * 640 + b0 + bw],
                        in_=bank[:, 0:bw], func=AF.Relu)
                    av = ACs.inc(i_ev)
                    bank_free[bname] = (ACs, av)
                    bank_free[bname[:-1] + "b"] = (ACs, av)
            w1buf_free[buf] = (PEs, PEs.n)
            if w1_next < len(w1_order):
                load_w1(*w1_order[w1_next], w1_next % 2)
                w1_next += 1
        m1_ac = ACs.n
        if e + 1 < NE:
            xe_fm_ready = emit_xe_trans(e + 1)
            if e + 2 < NE:
                gather_xe(e + 2)
        first_m2 = True
        for dhl in range(2):
            for fg in range(8):
                lsem, buf, ldv = w2_loads[(e, dhl, fg)]
                pe.wait_ge(lsem.h, ldv)
                if first_m2:
                    pe.wait_ge(ACs.h, m1_ac)
                    first_m2 = False
                w2base = buf * 4 * 512
                last = None
                for gi_ in range(NG):
                    bname, bank = m2_banks[gi_]
                    if fg == 0:
                        pe_guard(bname)
                    for fc in range(4):
                        fi = fg * 4 + fc
                        last = pe.matmul(
                            bank,
                            h_sb[:, fi * 640 + gi_ * 128: fi * 640 + (gi_ + 1) * 128],
                            w2_sb[:, w2base + fc * 512:w2base + (fc + 1) * 512],
                            start=(fg == 0 and fc == 0),
                            stop=(fg == 7 and fc == 3))
                w2buf_free[buf] = (PEs, PEs.inc(last))
                if w2_next < len(w2_order):
                    load_w2(*w2_order[w2_next], w2_next % 2)
                    w2_next += 1
            m2_pe = PEs.n
            dv.wait_ge(PEs.h, m2_pe)
            dv.wait_ge(SCT.h, sct_half_prev)
            for gi_ in range(NG):
                bname, bank = m2_banks[gi_]
                slot = gi_
                i_ev = dv.tensor_scalar(
                    out=ysc[:, slot * 512:(slot + 1) * 512], in0=bank,
                    scalar1=gate_sl[:, e * NG + gi_:e * NG + gi_ + 1],
                    scalar2=None, op0=ALU.mult)
                ev = DVs.inc(i_ev)
                bank_free[bname] = (DVs, ev)
                gp.wait_ge(DVs.h, ev)
                if e > 0:
                    gp.wait_ge(SCT.h, sct_expert_end[e - 1])
                else:
                    gp.wait_ge(ST.h, base_store)
                i_sc = gp.indirect_dma_start(
                    out=y_d.ap(),
                    out_offset=IndirectOffsetOnAxis(
                        toks_u32[:, e * NG + gi_:e * NG + gi_ + 1], 0),
                    in_=ysc[:, slot * 512:(slot + 1) * 512],
                    in_offset=None,
                    element_offset=dhl * 512,
                    compute_op=ALU.add)
                sct_hist.append(SCT.inc(i_sc, 16))
            sct_half_prev = SCT.n
        sct_expert_end[e] = SCT.n

    gp.wait_ge(SCT.h, SCT.n)
    for b in range(2):
        gp.wait_ge(W1G[b].h, W1G[b].n)
        gp.wait_ge(W2G[b].h, W2G[b].n)
    sy.wait_ge(SCT.h, SCT.n)
    sy.wait_ge(ST.h, ST.n)
    sy.wait_ge(LX.h, LX.n)
    sy.wait_ge(LD.h, LD.n)
    gp.wait_ge(LDW.h, LDW.n)
    gp.wait_ge(LDP.h, LDP.n)
    sy.wait_ge(LDO.h, LDO.n)
    gp.wait_ge(CP.h, CP.n)
    gp.wait_ge(SCI.h, SCI.n)
    gp.wait_ge(SCR.h, SCR.n)
    gp.wait_ge(GX.h, GX.n)
    for k in range(4):
        sy.wait_ge(XNB4[k].h, XNB4[k].n)
    for s in (W1S[0], W1S[1], W2S[0], W2S[1]):
        sy.wait_ge(s.h, s.n)
    es.close()
    return nc


_NC_CACHE = None


def kernel(L0_attn_x, L0_attn_Wqkv, L0_attn_bqkv, L0_attn_Wout, L0_attn_bout,
           L0_ffn_Wr, L0_ffn_W1, L0_ffn_W2):
    global _NC_CACHE
    x = np.ascontiguousarray(np.asarray(L0_attn_x, dtype=np.float32))
    wqkvT = np.ascontiguousarray(np.asarray(L0_attn_Wqkv, dtype=np.float32).T)
    bqkv = np.ascontiguousarray(np.asarray(L0_attn_bqkv, dtype=np.float32))
    woutT = np.ascontiguousarray(np.asarray(L0_attn_Wout, dtype=np.float32).T)
    bout = np.ascontiguousarray(np.asarray(L0_attn_bout, dtype=np.float32))
    wr = np.ascontiguousarray(np.asarray(L0_ffn_Wr, dtype=np.float32))
    w1 = np.ascontiguousarray(np.asarray(L0_ffn_W1, dtype=np.float32))
    w2 = np.ascontiguousarray(np.asarray(L0_ffn_W2, dtype=np.float32))

    if _NC_CACHE is None:
        _NC_CACHE = build_nc()
    nc = _NC_CACHE

    ident_np = np.eye(128, dtype=ml_dtypes.bfloat16)
    ones_np = np.ones((1, 128), dtype=ml_dtypes.bfloat16)
    iota8_np = np.tile(np.arange(8, dtype=np.float32), (128, 1))
    bandn_np = np.full((128, 640), -30000.0, dtype=np.float32)
    for r in range(128):
        bandn_np[r, r:r + WIN] = 0.0
    bandn_np = bandn_np.astype(ml_dtypes.bfloat16)
    tri_np = np.zeros((128, 128), dtype=ml_dtypes.bfloat16)
    for r in range(128):
        tri_np[r, r + 1:] = 1.0
    ones128_np = np.ones((128, 128), dtype=ml_dtypes.bfloat16)
    tokid_np = (np.arange(NT, dtype=np.float32)[None, :] * 128
                + np.arange(128, dtype=np.float32)[:, None])

    in_maps = []
    for b in range(8):
        in_maps.append({
            "x": x[b], "wqkvT": wqkvT, "bqkv": bqkv, "woutT": woutT,
            "bout": bout, "wr": wr, "w1": w1, "w2": w2,
            "ident_c": ident_np, "ones_c": ones_np, "iota8_c": iota8_np,
            "bandn_c": bandn_np, "tri_c": tri_np, "ones128_c": ones128_np,
            "tokid_c": tokid_np, "ident32_c": np.eye(128, dtype=np.float32),
        })
    res = run_bass_kernel_spmd(nc, in_maps, core_ids=list(range(8)))
    out = np.stack([res.results[b]["y"][:T] for b in range(8)], axis=0)
    return out


# version 18
# speedup vs baseline: 1.0667x; 1.0667x over previous
import numpy as np
import ml_dtypes
from contextlib import ExitStack

import concourse.bass as bass
import concourse.mybir as mybir
from concourse.bass_utils import run_bass_kernel_spmd

F32 = mybir.dt.float32
BF16 = mybir.dt.bfloat16
U32 = mybir.dt.uint32
AF = mybir.ActivationFunctionType
ALU = mybir.AluOpType
AX = mybir.AxisListType

T = 2048
D = 1024
NT = T // 128
KC = D // 128
NPAIR = 8
WIN = 512
FF = 4096
NE = 8


class Sem:
    def __init__(self, es, nc, name):
        self.h = es.enter_context(nc.semaphore(name))
        self.n = 0

    def inc(self, inst, k=1):
        self.n += k
        inst.then_inc(self.h, k)
        return self.n


def build_nc(debug=0):
    nc = bass.Bass("TRN2", target_bir_lowering=False,
                   detect_race_conditions=True)
    es = ExitStack()

    x_d = nc.dram_tensor("x", [T, D], F32, kind="ExternalInput")
    wqkvT_d = nc.dram_tensor("wqkvT", [D, 3 * D], F32, kind="ExternalInput")
    bqkv_d = nc.dram_tensor("bqkv", [3 * D], F32, kind="ExternalInput")
    woutT_d = nc.dram_tensor("woutT", [D, D], F32, kind="ExternalInput")
    bout_d = nc.dram_tensor("bout", [D], F32, kind="ExternalInput")
    wr_d = nc.dram_tensor("wr", [D, NE], F32, kind="ExternalInput")
    w1_d = nc.dram_tensor("w1", [NE, D, FF], F32, kind="ExternalInput")
    w2_d = nc.dram_tensor("w2", [NE, FF, D], F32, kind="ExternalInput")
    y_d = nc.dram_tensor("y", [T + 8, D], F32, kind="ExternalOutput")
    CAP = 640
    NG = CAP // 128
    xnb_d = nc.dram_tensor("xnb", [T + 8, D], BF16, kind="Internal")
    idxg_d = nc.dram_tensor("idxg", [NE * CAP + 128, 2], F32, kind="Internal")
    w1b_d = nc.dram_tensor("w1b", [NE, D, FF], BF16, kind="Internal")
    w2b_d = nc.dram_tensor("w2b", [NE, FF, D], BF16, kind="Internal")
    wqkvb_d = nc.dram_tensor("wqkvb", [D, 3 * D], BF16, kind="Internal")
    woutb_d = nc.dram_tensor("woutb", [D, D], BF16, kind="Internal")
    ident_c = nc.dram_tensor("ident_c", [128, 128], BF16, kind="ExternalInput")
    ones_c = nc.dram_tensor("ones_c", [1, 128], BF16, kind="ExternalInput")
    iota8_c = nc.dram_tensor("iota8_c", [128, 8], F32, kind="ExternalInput")
    bandn_c = nc.dram_tensor("bandn_c", [128, 640], BF16, kind="ExternalInput")
    tri_c = nc.dram_tensor("tri_c", [128, 128], BF16, kind="ExternalInput")
    ones128_c = nc.dram_tensor("ones128_c", [128, 128], BF16, kind="ExternalInput")
    tokid_c = nc.dram_tensor("tokid_c", [128, NT], F32, kind="ExternalInput")
    ident32_c = nc.dram_tensor("ident32_c", [128, 128], F32, kind="ExternalInput")

    SB = nc.alloc_sbuf_tensor
    y_sb = SB("y_sb", [128, NT * D], F32)
    xn_fm = SB("xn_fm", [128, KC * T], BF16)
    ident = SB("ident", [128, 128], BF16)
    ones_r = SB("ones_r", [1, 128], BF16)
    iota8 = SB("iota8", [128, 8], F32)
    bandn = SB("bandn", [128, 640], BF16)
    bq_sb = SB("bq_sb", [128, 24], F32)
    bq8_sb = SB("bq8_sb", [128, 8], F32)
    bout_r = SB("bout_r", [1, D], BF16)
    wr_sb = SB("wr_sb", [128, KC * NE], F32)
    ident32 = SB("ident32", [128, 128], F32)
    logit_sb = SB("logit_sb", [128, NT * 8], F32)
    zrow = SB("zrow", [1, D], BF16)
    gsel = SB("gsel", [128, 2 * NT], F32)
    probs = SB("probs", [128, NT * 8], F32)
    gates = SB("gates", [128, NT * 8], F32)
    s_mean = SB("s_mean", [128, NT], F32)
    s_aux = SB("s_aux", [128, 24], F32)
    s_rstd = SB("s_rstd", [128, NT], F32)
    s_ssq = SB("s_ssq", [128, NT], F32)
    s_m2 = SB("s_m2", [128, NT], F32)
    s_nmr = SB("s_nmr", [128, NT], F32)
    s_nmx = SB("s_nmx", [128, NT], F32)
    s_sme = SB("s_sme", [128, NT], F32)
    m8a = SB("m8a", [128, NT * 8], F32)
    i8a = SB("i8a", [128, NT * 8], U32)
    i8fa = SB("i8fa", [128, NT * 8], F32)
    mska = SB("mska", [128, NT * 8], F32)
    mskb = SB("mskb", [128, NT * 8], F32)
    mskc = SB("mskc", [128, NT * 8], F32)
    tri_sb = SB("tri_sb", [128, 128], BF16)
    ones128 = SB("ones128", [128, 128], BF16)
    tokid = SB("tokid", [128, NT], F32)
    flag_bf = SB("flag_bf", [128, NT * 8], BF16)
    pos_tm = SB("pos_tm", [128, NT * 8], F32)
    possel = SB("possel", [128, 4 * NT], F32)
    scat_src = SB("scat_src", [128, NT * 4], F32)
    dest_i = SB("dest_i", [128, NT * 2], U32)
    idx_all = SB("idx_all", [128, 8 * 5 * 2], F32)
    toks_u32 = SB("toks_u32", [128, 8 * 5], U32)
    gate_sl = SB("gate_sl", [128, 8 * 5], F32)
    zfill = SB("zfill", [128, 8 * 5 * 2], F32)

    ARENA = 98304
    arena = SB("arena", [128, ARENA], mybir.dt.uint8)
    abase = nc.lookup_mloc(arena).addr

    def AT(name, shape, dtype, off):
        return nc.alloc_sbuf_tensor_at(name, shape, dtype, offset=abase + off)

    o_all = AT("o_all", [128, NPAIR * T], BF16, 0)
    q_sb = AT("q_sb", [128, T], BF16, 32768)
    k_pad = AT("k_pad", [128, T + 512], BF16, 36864)
    v_st = AT("v_st", [128, T], BF16, 41984)
    v_tm = AT("v_tm", [128, T + 512], BF16, 46080)
    attnT0 = AT("attnT0", [128, 8 * 512], BF16, 51200)
    attnT1 = AT("attnT1", [128, 8 * 512], BF16, 59392)
    p_f32 = AT("p_f32", [128, 4 * 640], BF16, 67584)
    a_bf = AT("a_bf", [128, 4 * 640], BF16, 72704)
    wq_sb = AT("wq_sb", [128, KC * 128], BF16, 77824)
    wk_sb = AT("wk_sb", [128, KC * 128], BF16, 79872)
    wv_sb = AT("wv_sb", [128, KC * 128], BF16, 81920)
    wq2_sb = AT("wq2_sb", [128, KC * 128], BF16, 92160)
    wk2_sb = AT("wk2_sb", [128, KC * 128], BF16, 94208)
    wv2_sb = AT("wv2_sb", [128, KC * 128], BF16, 96256)
    woutT_sb = AT("woutT_sb", [128, KC * 512], BF16, 83968)
    junk_sb = AT("junk_sb", [128, D], BF16, 0)
    xn_sts = [AT(f"xn_st{k}", [128, D], F32, 2048 + 4096 * k) for k in range(4)]
    xnb_sts = [AT(f"xnb_st{k}", [128, D], BF16, 18432 + 2048 * k) for k in range(4)]
    xn32b = AT("xn32b", [128, 2 * D], F32, 26624)
    h_sb = AT("h_sb", [128, 32 * 640], BF16, 0)
    xe_tm = AT("xe_tm", [128, 5 * 1024], BF16, 40960)
    xe_fm = AT("xe_fm", [128, KC * 640], BF16, 51200)
    w1_sb = AT("w1_sb", [128, 2 * KC * 512], BF16, 61440)
    w2_sb = AT("w2_sb", [128, 2 * 4 * 512], BF16, 77824)
    ysc = AT("ysc", [128, 5 * 512], F32, 86016)

    PS = nc.alloc_psum_tensor
    pb0 = PS("pb0", [128, 512], F32)
    pb1 = PS("pb1", [128, 512], F32)
    pbS0 = PS("pbS0", [128, 1024], F32)
    pbS1 = PS("pbS1", [128, 1024], F32)
    pT0 = PS("pT0", [128, 1024], BF16)
    pT1 = PS("pT1", [128, 1024], BF16)

    LD = Sem(es, nc, "LD")
    LDW = Sem(es, nc, "LDW")
    CP = Sem(es, nc, "CP")
    LDO = Sem(es, nc, "LDO")
    LDP = Sem(es, nc, "LDP")
    W1S = [Sem(es, nc, "W1A"), Sem(es, nc, "W1B")]
    W2S = [Sem(es, nc, "W2A"), Sem(es, nc, "W2B")]
    W1G = [Sem(es, nc, "W1GA"), Sem(es, nc, "W1GB")]
    W2G = [Sem(es, nc, "W2GA"), Sem(es, nc, "W2GB")]
    LX = Sem(es, nc, "LX")
    ST = Sem(es, nc, "ST")
    XNB4 = [Sem(es, nc, f"XNB{k}") for k in range(4)]
    XQ = [Sem(es, nc, f"XQ{k}") for k in range(4)]
    SCI = Sem(es, nc, "SCI")
    SCR = Sem(es, nc, "SCR")
    GX = Sem(es, nc, "GX")
    SCT = Sem(es, nc, "SCT")
    PEs = Sem(es, nc, "PEs")
    ACs = Sem(es, nc, "ACs")
    DVs = Sem(es, nc, "DVs")
    ALL_SEMS = [LD, LDW, LDO, LDP, CP, W1S[0], W1S[1], W2S[0], W2S[1],
                W1G[0], W1G[1], W2G[0], W2G[1],
                LX, ST, SCI, SCR, GX, SCT, PEs, ACs, DVs] + XNB4 + XQ

    pe, ac, dv, sy, gp = nc.tensor, nc.scalar, nc.vector, nc.sync, nc.gpsimd

    for s in ALL_SEMS:
        sy.sem_clear(s.h)
    nc.all_engine_barrier()

    bank_free = {}

    def pe_guard(name):
        if name in bank_free:
            s, v = bank_free[name]
            pe.wait_ge(s.h, v)

    LX.inc(sy.dma_start(out=ident[:], in_=ident_c[:, :]), 16)
    LX.inc(sy.dma_start(out=ones_r[0:1, :], in_=ones_c[:, :]), 16)
    LX.inc(sy.dma_start(out=iota8[:], in_=iota8_c[:, :]), 16)
    LX.inc(sy.dma_start(out=bandn[:], in_=bandn_c[:, :]), 16)
    LX.inc(sy.dma_start(out=bq_sb[:], in_=bqkv_d.ap().rearrange("(c p) -> p c", p=128),
                        allow_slow_non_contiguous=True), 16)
    LDW.inc(gp.dma_start(out=bout_r[0:1, :], in_=bout_d.ap().rearrange("(o n) -> o n", o=1)), 16)
    LX.inc(sy.dma_start(out=tri_sb[:], in_=tri_c[:, :]), 16)
    LX.inc(sy.dma_start(out=ones128[:], in_=ones128_c[:, :]), 16)
    LX.inc(sy.dma_start(out=tokid[:], in_=tokid_c[:, :]), 16)
    LX.inc(sy.dma_start(out=ident32[:], in_=ident32_c[:, :]), 16)
    LX.inc(sy.dma_start(out=wr_sb.ap().rearrange("p (c e) -> p c e", e=8),
                        in_=wr_d.ap().rearrange("(c p) e -> p c e", p=128),
                        allow_slow_non_contiguous=True), 16)
    lx_all = LX.n
    y_sb3 = y_sb.ap().rearrange("p (i d) -> p i d", d=D)
    x3 = x_d.ap().rearrange("(i p) d -> p i d", p=128)
    for q in range(4):
        eng = sy if q % 2 == 0 else ac
        XQ[q].inc(eng.dma_start(out=y_sb3[:, q * 4:(q + 1) * 4, :],
                                in_=x3[:, q * 4:(q + 1) * 4, :]), 16)

    for hp0 in (0, 1):
        dsts = (wq_sb, wk_sb, wv_sb) if hp0 == 0 else (wq2_sb, wk2_sb, wv2_sb)
        for (wsb, col0) in zip(dsts, (hp0 * 128, D + hp0 * 128, 2 * D + hp0 * 128)):
            gp.wait_ge(LDP.h, LDP.n)
            srcw = wqkvT_d[:, col0:col0 + 128].rearrange("(c p) o -> p c o", p=128)
            LDP.inc(gp.dma_start(out=wsb.ap().rearrange("p (c o) -> p c o", o=128),
                                 in_=srcw), 16)
    ld_pair01 = LDP.n
    cast_done = {}
    CP.inc(gp.dma_start(out=wqkvb_d[:, :], in_=wqkvT_d[:, :]), 16)
    cast_qkv = CP.n
    gp.wait_ge(CP.h, CP.n)
    CP.inc(gp.dma_start(out=woutb_d[:, :], in_=woutT_d[:, :]), 16)
    cast_wout = CP.n

    pe.wait_ge(LX.h, lx_all)

    ys3 = y_sb.ap().rearrange("p (i d) -> p i d", d=D)
    sq_ac = {}
    nmr_dv = {}
    gates_done = []
    ln_ac_hist = []
    trs_pv1 = []
    xn_done_dv = []
    router_pe = {}

    sq_prev = [None]

    def emit_stats(q):
        sl = slice(q * 4, q * 4 + 4)
        ac.wait_ge(XQ[q].h, 16)
        for i in range(q * 4, q * 4 + 4):
            if sq_prev[0] is not None:
                ac.wait_ge(ACs.h, sq_prev[0])
            i_sq = ac.activation(out=junk_sb[:], in_=y_sb[:, i * D:(i + 1) * D],
                                 func=AF.Square, accum_out=s_ssq[:, i:i + 1])
            sq_prev[0] = ACs.inc(i_sq)
        sq_ac[q] = sq_prev[0]
        dv.wait_ge(XQ[q].h, 16)
        dv.reduce_sum(s_mean[:, sl], ys3[:, sl, :], axis=AX.X)
        dv.drain()
        dv.tensor_scalar(out=s_mean[:, sl], in0=s_mean[:, sl], scalar1=1.0 / D,
                         scalar2=None, op0=ALU.mult)
        dv.drain()
        dv.tensor_mul(s_m2[:, sl], s_mean[:, sl], s_mean[:, sl])
        dv.drain()
        dv.tensor_scalar(out=s_m2[:, sl], in0=s_m2[:, sl], scalar1=1.0e-5,
                         scalar2=None, op0=ALU.subtract)
        dv.wait_ge(ACs.h, sq_ac[q])
        dv.drain()
        i_t = dv.scalar_tensor_tensor(out=s_rstd[:, sl], in0=s_ssq[:, sl],
                                      scalar=1.0 / D, in1=s_m2[:, sl],
                                      op0=ALU.mult, op1=ALU.subtract)
        DVs.inc(i_t)
        ac.wait_ge(DVs.h, DVs.n)
        i_r = ac.activation(out=s_rstd[:, sl], in_=s_rstd[:, sl], func=AF.Sqrt)
        ACs.inc(i_r)
        dv.wait_ge(ACs.h, ACs.n)
        dv.reciprocal(s_rstd[:, sl], s_rstd[:, sl])
        dv.drain()
        i_n = dv.scalar_tensor_tensor(out=s_nmr[:, sl], in0=s_mean[:, sl],
                                      scalar=-1.0, in1=s_rstd[:, sl],
                                      op0=ALU.mult, op1=ALU.mult)
        nmr_dv[q] = DVs.inc(i_n)

    def emit_router(i):
        base = (i % 2) * D
        pe.wait_ge(DVs.h, xn_done_dv[i])
        last = None
        for c in range(KC):
            last = pe.matmul(pbS0[:, i * 8:(i + 1) * 8],
                             xn32b[:, base + c * 128: base + (c + 1) * 128],
                             wr_sb[:, c * 8:(c + 1) * 8],
                             start=(c == 0), stop=(c == KC - 1))
        router_pe[i] = PEs.inc(last)

    def emit_tile(i):
        s4 = i % 4
        xn_st = xn_sts[s4]
        xnb_st = xnb_sts[s4]
        ys = y_sb[:, i * D:(i + 1) * D]
        if s4 == 0:
            ac.wait_ge(DVs.h, nmr_dv[i // 4])
        if i >= 4:
            ac.wait_ge(PEs.h, trs_pv1[i - 4])
            ac.wait_ge(XNB4[s4].h, XNB4[s4].n)
        i_xn = ac.activation(out=xn_st[:], in_=ys, func=AF.Identity,
                             scale=s_rstd[:, i:i + 1], bias=s_nmr[:, i:i + 1])
        i_xnb = ac.activation(out=xnb_st[:], in_=ys, func=AF.Identity,
                              scale=s_rstd[:, i:i + 1], bias=s_nmr[:, i:i + 1])
        ln_ac_hist.append(ACs.inc(i_xnb))
        sy.wait_ge(ACs.h, ln_ac_hist[i])
        XNB4[s4].inc(sy.dma_start(out=xnb_d[i * 128:(i + 1) * 128, :],
                                  in_=xnb_st[:]), 16)
        base = (i % 2) * D
        pe.wait_ge(ACs.h, ln_ac_hist[i])
        pvs = []
        for ghalf, bname in ((0, "pbS1a"), (1, "pbS1b")):
            bslice = pbS1[:, 0:512] if ghalf == 0 else pbS1[:, 512:1024]
            pe_guard(bname)
            last = None
            for cc in range(4):
                c = ghalf * 4 + cc
                last = pe.transpose(bslice[:, cc * 128:(cc + 1) * 128],
                                    xn_st[:, c * 128:(c + 1) * 128], ident32[:])
            pvs.append(PEs.inc(last))
        trs_pv1.append(PEs.n)
        if i > 0:
            emit_router(i - 1)
        for ghalf, bname in ((0, "pbS1a"), (1, "pbS1b")):
            bslice = pbS1[:, 0:512] if ghalf == 0 else pbS1[:, 512:1024]
            dv.wait_ge(PEs.h, pvs[ghalf])
            dst = bass.AP(xn_fm, ghalf * 4 * T + i * 128,
                          [[KC * T, 128], [T, 4], [1, 128]])
            dv.tensor_copy(dst, bslice.rearrange("p (c q) -> p c q", q=128))
            i_ev = dv.tensor_copy(
                xn32b[:, base + ghalf * 512: base + (ghalf + 1) * 512], bslice)
            ev = DVs.inc(i_ev)
            bank_free[bname] = (DVs, ev)
        xn_done_dv.append(DVs.n)

    emit_stats(0)
    for q in range(4):
        if q + 1 < 4:
            emit_stats(q + 1)
        for i in range(q * 4, q * 4 + 4):
            emit_tile(i)
    emit_router(NT - 1)
    xn_all_dv = max(xn_done_dv)

    dv.wait_ge(LX.h, lx_all)
    i_b = dv.tensor_scalar(out=bq8_sb[:], in0=bq_sb[:, 0:8], scalar1=0.125,
                           scalar2=None, op0=ALU.mult)
    bq8_dv = DVs.inc(i_b)
    dv.wait_ge(PEs.h, router_pe[NT - 1])
    i_lc = dv.tensor_copy(logit_sb[:], pbS0[:, 0:NT * 8])
    lc_dv = DVs.inc(i_lc)
    bank_free["pbS0a"] = (DVs, lc_dv)
    dv.drain()

    i_m = dv.reduce_max(s_nmx[:], logit_sb.ap().rearrange("p (i e) -> p i e", e=8),
                        axis=AX.X, negate=True)
    DVs.inc(i_m)
    ac.wait_ge(DVs.h, DVs.n)
    last_e = None
    for i in range(NT):
        last_e = ac.activation(out=probs[:, i * 8:(i + 1) * 8],
                               in_=logit_sb[:, i * 8:(i + 1) * 8], func=AF.Exp,
                               bias=s_nmx[:, i:i + 1],
                               accum_out=s_sme[:, i:i + 1])
    ACs.inc(last_e)
    dv.wait_ge(ACs.h, ACs.n)
    dv.reciprocal(s_sme[:], s_sme[:])
    dv.drain()
    for i in range(NT):
        dv.tensor_scalar(out=probs[:, i * 8:(i + 1) * 8],
                         in0=probs[:, i * 8:(i + 1) * 8],
                         scalar1=s_sme[:, i:i + 1], scalar2=None, op0=ALU.mult)
    dv.drain()
    for i in range(NT):
        dv.max(out=m8a[:, i * 8:(i + 1) * 8], in_=probs[:, i * 8:(i + 1) * 8])
    dv.drain()
    for i in range(NT):
        dv.max_index(i8a[:, i * 8:(i + 1) * 8], m8a[:, i * 8:(i + 1) * 8],
                     probs[:, i * 8:(i + 1) * 8])
    dv.drain()
    dv.tensor_copy(i8fa[:], i8a[:])
    dv.drain()
    for i in range(NT):
        dv.tensor_scalar(out=mska[:, i * 8:(i + 1) * 8], in0=iota8[:],
                         scalar1=i8fa[:, i * 8:i * 8 + 1], scalar2=None,
                         op0=ALU.is_equal)
        dv.tensor_scalar(out=mskb[:, i * 8:(i + 1) * 8], in0=iota8[:],
                         scalar1=i8fa[:, i * 8 + 1:i * 8 + 2], scalar2=None,
                         op0=ALU.is_equal)
    dv.drain()
    dv.tensor_add(mskc[:], mska[:], mskb[:])
    dv.drain()
    i_g = dv.tensor_mul(gates[:], mskc[:], probs[:])
    gv = DVs.inc(i_g)
    gates_done = [gv] * NT
    dv.drain()
    dv.tensor_mul(mskc[:], mska[:], probs[:])
    dv.drain()
    dv.reduce_sum(gsel.ap().rearrange("p (k i) -> p k i", k=2)[:, 0:1, :],
                  mskc.ap().rearrange("p (i e) -> p i e", e=8), axis=AX.X)
    dv.drain()
    dv.tensor_mul(mskc[:], mskb[:], probs[:])
    dv.drain()
    dv.reduce_sum(gsel.ap().rearrange("p (k i) -> p k i", k=2)[:, 1:2, :],
                  mskc.ap().rearrange("p (i e) -> p i e", e=8), axis=AX.X)
    dv.drain()
    i_f = dv.tensor_scalar(out=flag_bf[:], in0=gates[:], scalar1=0.0, scalar2=None,
                           op0=ALU.is_gt)
    flag_dv = DVs.inc(i_f)

    xn_all_dv = max(xn_done_dv)

    if debug:
        dbg_xn = nc.dram_tensor("dbg_xn", [128, KC * T], F32, kind="ExternalOutput")
        dbg_pr = nc.dram_tensor("dbg_pr", [128, NT * 8], F32, kind="ExternalOutput")
        dbg_ga = nc.dram_tensor("dbg_ga", [128, NT * 8], F32, kind="ExternalOutput")
        sy.wait_ge(DVs.h, DVs.n)
        gp.wait_ge(DVs.h, DVs.n)
        LD.inc(gp.dma_start(out=dbg_xn[:, :], in_=xn_fm[:]), 16)
        ST.inc(sy.dma_start(out=dbg_pr[:, :], in_=probs[:]), 16)
        ST.inc(sy.dma_start(out=dbg_ga[:, :], in_=gates[:]), 16)
    if debug == 1:
        ST.inc(sy.dma_start(out=y_d[0:T, :].rearrange("(i p) d -> p i d", p=128),
                            in_=y_sb.ap().rearrange("p (i d) -> p i d", d=D)), 16)
        sy.wait_ge(ST.h, ST.n)
        sy.wait_ge(LX.h, LX.n)
        gp.wait_ge(LD.h, LD.n)
        gp.wait_ge(LDW.h, LDW.n)
        gp.wait_ge(CP.h, CP.n)
        es.close()
        return nc

    from concourse.bass_core_rust import IndirectOffsetOnAxis

    def emit_compaction():
        pe.wait_ge(DVs.h, flag_dv)
        pe_guard("pb0")
        last = None
        for i in range(NT):
            for c in range(i + 1):
                last = pe.matmul(pb0[:, i * 8:(i + 1) * 8],
                                 tri_sb[:] if c == i else ones128[:],
                                 flag_bf[:, c * 8:(c + 1) * 8],
                                 start=(i == 0 and c == 0),
                                 stop=(i == NT - 1 and c == i))
        scan_pe = PEs.inc(last)
        dv.wait_ge(PEs.h, scan_pe)
        i_ev = dv.tensor_copy(pos_tm[:], pb0[:, 0:NT * 8])
        scan_dv = DVs.inc(i_ev)
        bank_free["pb0"] = (DVs, scan_dv)
        dv.drain()
        for i in range(NT):
            p8 = pos_tm[:, i * 8:(i + 1) * 8]
            dv.tensor_mul(mska[:, i * 8:(i + 1) * 8], mska[:, i * 8:(i + 1) * 8], p8)
            dv.tensor_mul(mskb[:, i * 8:(i + 1) * 8], mskb[:, i * 8:(i + 1) * 8], p8)
        dv.drain()
        pos3 = possel.ap().rearrange("p (k i) -> p k i", k=4)
        dv.reduce_sum(pos3[:, 0:1, :], mska.ap().rearrange("p (i e) -> p i e", e=8),
                      axis=AX.X, apply_transpose=False)
        dv.reduce_sum(pos3[:, 1:2, :], mskb.ap().rearrange("p (i e) -> p i e", e=8),
                      axis=AX.X, apply_transpose=False)
        dv.drain()
        ea = mskc[:, 0:2 * NT]
        i8v = i8fa.ap().rearrange("p (i e) -> p i e", e=8)
        for ch in range(2):
            dv.tensor_copy(ea[:, ch * NT:(ch + 1) * NT], i8v[:, :, ch:ch + 1])
        dv.drain()
        pos2 = possel[:, 0:2 * NT]
        dst2 = possel[:, 2 * NT:4 * NT]
        dv.scalar_tensor_tensor(out=dst2, in0=ea, scalar=float(CAP), in1=pos2,
                                op0=ALU.mult, op1=ALU.add)
        dv.drain()
        dv.tensor_scalar(out=pos2, in0=pos2, scalar1=float(CAP), scalar2=None,
                         op0=ALU.is_lt)
        dv.drain()
        dv.scalar_tensor_tensor(out=dst2, in0=dst2, scalar=1.0, in1=pos2,
                                op0=ALU.mult, op1=ALU.mult)
        dv.drain()
        dv.scalar_tensor_tensor(out=pos2, in0=pos2, scalar=-float(NE * CAP),
                                in1=dst2, op0=ALU.mult, op1=ALU.add)
        dv.drain()
        dv.tensor_scalar(out=dst2, in0=pos2, scalar1=float(NE * CAP),
                         scalar2=None, op0=ALU.add)
        for ch in range(2):
            dv.tensor_copy(bass.AP(scat_src, 2 * ch, [[64, 128], [4, NT], [1, 1]]),
                           tokid.ap().rearrange("p (i o) -> p i o", o=1))
            dv.tensor_copy(bass.AP(scat_src, 2 * ch + 1, [[64, 128], [4, NT], [1, 1]]),
                           gsel[:, ch * NT:(ch + 1) * NT].rearrange("p (i o) -> p i o", o=1))
        dv.drain()
        dv.tensor_copy(dest_i[:], possel[:, 2 * NT:4 * NT])
        dv.drain()
        zf3 = zfill.ap().rearrange("p (e g b) -> p e g b", b=2, g=NG)
        for g in range(NG):
            dv.memset(zf3[:, :, g:g + 1, 0:1], float(T + g))
        dv.memset(zf3[:, :, :, 1:2], 0.0)
        i_zr = dv.memset(zrow[0:1, :], 0.0)
        route_dv = DVs.inc(i_zr)

        gp.wait_ge(DVs.h, route_dv)
        SCI.inc(gp.dma_start(out=idxg_d[0:NE * CAP, :].rearrange("(e g p) b -> p e g b", p=128, g=NG),
                             in_=zfill.ap().rearrange("p (e g b) -> p e g b", b=2, g=NG)), 16)
        for tg in range(8):
            SCI.inc(gp.dma_start(out=xnb_d[T + tg:T + tg + 1, :], in_=zrow[0:1, :]), 16)
        gp.wait_ge(SCI.h, SCI.n)
        idxg_flat = idxg_d.ap()
        for i in range(NT):
            for ch in range(2):
                col = (i * 2 + ch) * 2
                SCI.inc(gp.indirect_dma_start(
                    out=idxg_flat,
                    out_offset=IndirectOffsetOnAxis(dest_i[:, NT * ch + i:NT * ch + i + 1], 0),
                    in_=scat_src[:, col:col + 2], in_offset=None), 16)
        gp.wait_ge(SCI.h, SCI.n)
        SCR.inc(gp.dma_start(
            out=idx_all.ap().rearrange("p (e g b) -> p e g b", b=2, g=NG),
            in_=idxg_d[0:NE * CAP, :].rearrange("(e g p) b -> p e g b", p=128, g=NG)), 16)

    def emit_route2():
        dv.wait_ge(SCR.h, SCR.n)
        ia3 = idx_all.ap().rearrange("p (s b) -> p s b", b=2)
        dv.tensor_copy(toks_u32[:], ia3[:, :, 0:1])
        i_gs = dv.tensor_copy(gate_sl[:], ia3[:, :, 1:2])
        return DVs.inc(i_gs)

    def pair_bufs(hp):
        return (wq_sb, wk_sb, wv_sb) if hp % 2 == 0 else (wq2_sb, wk2_sb, wv2_sb)

    def load_pair_w(hp):
        sy.wait_ge(CP.h, cast_qkv)
        for (wsb, col0) in zip(pair_bufs(hp), (hp * 128, D + hp * 128,
                                               2 * D + hp * 128)):
            src = wqkvb_d[:, col0:col0 + 128].rearrange("(c p) o -> p c o", p=128)
            LD.inc(sy.dma_start(out=wsb.ap().rearrange("p (c o) -> p c o", o=128),
                                in_=src), 16)
        return LD.n

    def load_woutT(dhl):
        sy.wait_ge(CP.h, cast_wout)
        src = woutb_d[:, dhl * 512:(dhl + 1) * 512] \
            .rearrange("(c p) o -> p c o", p=128)
        return LDO.inc(sy.dma_start(
            out=woutT_sb.ap().rearrange("p (c o) -> p c o", o=512), in_=src), 16)

    dv.memset(k_pad[:, 0:256], 0.0)
    dv.memset(k_pad[:, T + 256:T + 512], 0.0)
    dv.memset(v_tm[:, 0:256], 0.0)
    i_pad = dv.memset(v_tm[:, T + 256:T + 512], 0.0)
    pads_dv = DVs.inc(i_pad)

    last_av_pe = 0
    abf_slot_prev = {}
    ldw_for = {0: ld_pair01, 1: ld_pair01}

    for k in range(4):
        ac.wait_ge(XNB4[k].h, XNB4[k].n)
    for hp in range(NPAIR):
        if hp < 2:
            pe.wait_ge(LDP.h, ld_pair01)
        else:
            pe.wait_ge(LD.h, LD.n)
        if hp == 0:
            pe.wait_ge(DVs.h, xn_all_dv)
            ac.wait_ge(DVs.h, bq8_dv)
        wqp, wkp, wvp = pair_bufs(hp)
        qkv_sets = ((wvp, v_st, 16 + hp, 0), (wqp, q_sb, hp, 0),
                    (wkp, k_pad, 8 + hp, 256))
        for (wsb, dst, bidx, off) in qkv_sets:
            for tb in range(4):
                bname = f"pb{tb % 2}"
                bank = pb0 if tb % 2 == 0 else pb1
                pe_guard(bname)
                last = None
                for c in range(KC):
                    last = pe.matmul(bank[:], wsb[:, c * 128:(c + 1) * 128],
                                     xn_fm[:, c * T + tb * 512: c * T + (tb + 1) * 512],
                                     start=(c == 0), stop=(c == KC - 1))
                pv = PEs.inc(last)
                ac.wait_ge(PEs.h, pv)
                d0 = tb * 512 + off
                if wsb is wqp:
                    i_ev = ac.activation(out=dst[:, d0:d0 + 512], in_=bank[:],
                                         func=AF.Identity, scale=0.125,
                                         bias=bq8_sb[:, hp:hp + 1])
                else:
                    i_ev = ac.activation(out=dst[:, d0:d0 + 512], in_=bank[:],
                                         func=AF.Identity,
                                         bias=bq_sb[:, bidx:bidx + 1])
                av = ACs.inc(i_ev)
                bank_free[bname] = (ACs, av)
            if wsb is wvp:
                v_ac = ACs.n
        qkv_ac = ACs.n
        pe.wait_ge(ACs.h, v_ac)
        if hp == 0:
            pe.wait_ge(DVs.h, pads_dv)
        for g in range(2):
            bname = f"pT{g}"
            bankT = pT0 if g == 0 else pT1
            pe_guard(bname)
            last = None
            for j in range(8):
                jj = g * 8 + j
                last = pe.transpose(bankT[:, j * 128:(j + 1) * 128],
                                    v_st[:, jj * 128:(jj + 1) * 128], ident[:])
            pv = PEs.inc(last)
            dv.wait_ge(PEs.h, pv)
            i_ev = dv.tensor_copy(v_tm[:, 256 + g * 1024: 256 + (g + 1) * 1024],
                                  bankT[:])
            ev = DVs.inc(i_ev)
            bank_free[bname] = (DVs, ev)

        if hp + 2 < NPAIR:
            sy.wait_ge(PEs.h, PEs.n)
            ldw_for[hp + 2] = load_pair_w(hp + 2)
        elif hp == NPAIR - 2:
            sy.wait_ge(PEs.h, PEs.n)
            load_woutT(0)
        if hp == 0:
            emit_compaction()
        pe.wait_ge(ACs.h, qkv_ac)

        attn_ready = {}
        trs_done = {}
        sums_of = {}

        def aux_of(i, base, h):
            sl = (i % 2) * 2 + h
            return s_aux[:, base + sl:base + sl + 1]

        def emit_norm(i):
            dv.wait_ge(ACs.h, sums_of[i])
            i_rc = None
            for h in (0, 1):
                i_rc = dv.reciprocal(aux_of(i, 20, h), aux_of(i, 16, h))
            rc = DVs.inc(i_rc)
            ac.wait_ge(DVs.h, rc)
            par = i % 2
            if par in abf_slot_prev:
                ac.wait_ge(PEs.h, abf_slot_prev[par])
            for h in (0, 1):
                po = (par * 2 + h) * 640
                i_n = ac.activation(out=a_bf[:, po:po + 640],
                                    in_=p_f32[:, po:po + 640],
                                    func=AF.Identity, scale=aux_of(i, 20, h))
                attn_ready[(i, h)] = (ACs, ACs.inc(i_n))

        def emit_scores(i, prev):
            par = i % 2
            sc = []
            for (h, bS, row0) in ((0, pbS0, 0), (1, pbS1, 64)):
                pe_guard(f"pbS{h}a"); pe_guard(f"pbS{h}b")
                lhs = q_sb[row0:row0 + 64, i * 128:(i + 1) * 128]
                rhs0 = k_pad[row0:row0 + 64, i * 128: i * 128 + 512]
                rhs1 = k_pad[row0:row0 + 64, i * 128 + 512: i * 128 + 640]
                tp = (row0, 0)
                pe.matmul(bS[:, 0:512], lhs, rhs0, start=True, stop=False,
                          tile_position=tp)
                pe.matmul(bS[:, 512:640], lhs, rhs1, start=True, stop=False,
                          tile_position=tp)
                pe.matmul(bS[:, 0:512], ident[:], bandn[:, 0:512],
                          start=False, stop=True)
                last = pe.matmul(bS[:, 512:640], ident[:], bandn[:, 512:640],
                                 start=False, stop=True)
                sc.append(PEs.inc(last))
            rc = None
            if prev is not None:
                dv.wait_ge(ACs.h, sums_of[prev])
                i_rc = None
                for h in (0, 1):
                    i_rc = dv.reciprocal(aux_of(prev, 20, h), aux_of(prev, 16, h))
                rc = DVs.inc(i_rc)
                ppar = prev % 2
                ac.wait_ge(DVs.h, rc)
                if ppar in abf_slot_prev:
                    ac.wait_ge(PEs.h, abf_slot_prev[ppar])
                po = ppar * 2 * 640
                i_n = ac.activation(out=a_bf[:, po:po + 640],
                                    in_=p_f32[:, po:po + 640],
                                    func=AF.Identity, scale=aux_of(prev, 20, 0))
                attn_ready[(prev, 0)] = (ACs, ACs.inc(i_n))
            dv.wait_ge(PEs.h, sc[0])
            i_m = dv.reduce_max(aux_of(i, 12, 0), pbS0[:, 0:640], axis=AX.X,
                                negate=True)
            m0 = DVs.inc(i_m)
            ac.wait_ge(DVs.h, m0)
            po = (par * 2) * 640
            i_e = ac.activation(out=p_f32[:, po:po + 640], in_=pbS0[:, 0:640],
                                func=AF.Exp, bias=aux_of(i, 12, 0),
                                accum_out=aux_of(i, 16, 0))
            e0 = ACs.inc(i_e)
            bank_free["pbS0a"] = (ACs, e0)
            bank_free["pbS0b"] = (ACs, e0)
            dv.wait_ge(PEs.h, sc[1])
            i_m = dv.reduce_max(aux_of(i, 12, 1), pbS1[:, 0:640], axis=AX.X,
                                negate=True)
            m1 = DVs.inc(i_m)
            if prev is not None:
                ppar = prev % 2
                dv.wait_ge(DVs.h, rc)
                if ppar in abf_slot_prev:
                    dv.wait_ge(PEs.h, abf_slot_prev[ppar])
                po = (ppar * 2 + 1) * 640
                i_n = dv.tensor_scalar(out=a_bf[:, po:po + 640],
                                       in0=p_f32[:, po:po + 640],
                                       scalar1=aux_of(prev, 20, 1), scalar2=None,
                                       op0=ALU.mult)
                attn_ready[(prev, 1)] = (DVs, DVs.inc(i_n))
            ac.wait_ge(DVs.h, m1)
            po = (par * 2 + 1) * 640
            i_e = ac.activation(out=p_f32[:, po:po + 640], in_=pbS1[:, 0:640],
                                func=AF.Exp, bias=aux_of(i, 12, 1),
                                accum_out=aux_of(i, 16, 1))
            sums_of[i] = ACs.inc(i_e)
            bank_free["pbS1a"] = (ACs, sums_of[i])
            bank_free["pbS1b"] = (ACs, sums_of[i])

        def emit_transposes(i):
            par = i % 2
            r = i % 4
            for (h, aT) in ((0, attnT0), (1, attnT1)):
                po = (par * 2 + h) * 640
                bname = f"pT{h}"
                bankT = pT0 if h == 0 else pT1
                pe_guard(bname)
                rs, rv = attn_ready[(i, h)]
                pe.wait_ge(rs.h, rv)
                last = None
                for c in range(5):
                    last = pe.transpose(bankT[:, c * 128:(c + 1) * 128],
                                        a_bf[:, po + c * 128:po + (c + 1) * 128],
                                        ident[:])
                pv = PEs.inc(last)
                dv.wait_ge(PEs.h, pv)
                dst = bass.AP(aT, r * 640, [[8 * 512, 128], [512, 5], [1, 128]])
                i_ev = dv.tensor_copy(
                    dst, bankT[:, 0:640].rearrange("p (c q) -> p c q", q=128))
                ev = DVs.inc(i_ev)
                bank_free[bname] = (DVs, ev)
            trs_done[i] = PEs.n
            abf_slot_prev[par] = PEs.n
            return DVs.n

        def emit_av(qb, trs_dv):
            pe_guard("pb0")
            pe_guard("pb1")
            pe.wait_ge(DVs.h, trs_dv)
            last = None
            d_order = [4, 0, 1, 2, 3, 5, 6, 7]
            for d in d_order:
                j = 4 * qb + d
                q0, q1 = max(0, d - 4) * 128, (min(3, d) + 1) * 128
                for (h, aT, bk, cp) in ((0, attnT0, pb0, 0), (1, attnT1, pb1, 64)):
                    last = pe.matmul(bk[cp:cp + 64, q0:q1],
                                     v_tm[:, j * 128 + cp: j * 128 + cp + 64],
                                     aT[:, d * 512 + q0: d * 512 + q1],
                                     start=(d == 4), stop=(d == 7),
                                     tile_position=(0, cp))
            av_pe = PEs.inc(last)
            ac.wait_ge(PEs.h, av_pe)
            ocol = hp * T + qb * 512
            ac.copy(out=o_all[0:64, ocol:ocol + 512], in_=pb0[0:64, :])
            i_ev = ac.copy(out=o_all[64:128, ocol:ocol + 512], in_=pb1[64:128, :])
            av_ac = ACs.inc(i_ev)
            bank_free["pb0"] = (ACs, av_ac)
            bank_free["pb1"] = (ACs, av_ac)
            return av_pe

        order = [3, 0, 1, 2]
        seq = [qb * 4 + r for qb in range(4) for r in order]
        tdv = 0
        for k in range(len(seq)):
            if k >= 2:
                tdv = emit_transposes(seq[k - 2])
            emit_scores(seq[k], seq[k - 1] if k >= 1 else None)
            if k >= 2 and (k - 2) % 4 == 3:
                last_av_pe = attn_done_pe = emit_av((k - 2) // 4, tdv)
        emit_norm(seq[15])
        tdv = emit_transposes(seq[14])
        tdv = emit_transposes(seq[15])
        last_av_pe = attn_done_pe = emit_av(3, tdv)
        if hp == 0:
            route2_dv = emit_route2()
            for e in range(5):
                gp.wait_ge(CP.h, CP.n)
                CP.inc(gp.dma_start(out=w1b_d[e, :, :], in_=w1_d[e, :, :]), 16)
                gp.wait_ge(CP.h, CP.n)
                CP.inc(gp.dma_start(out=w2b_d[e, :, :], in_=w2_d[e, :, :]), 16)
                cast_done[e] = CP.n

    gp.wait_ge(DVs.h, route2_dv)
    for k in range(4):
        gp.wait_ge(XNB4[k].h, XNB4[k].n)
    gp.wait_ge(PEs.h, attn_done_pe)
    for g in range(5):
        GX.inc(gp.indirect_dma_start(
            out=xe_tm[:, g * 1024:(g + 1) * 1024],
            out_offset=None,
            in_=xnb_d.ap(),
            in_offset=IndirectOffsetOnAxis(toks_u32[:, g:g + 1], 0)), 16)
    gather0_done = GX.n

    w1buf_free = [(PEs, attn_done_pe), (PEs, attn_done_pe)]
    w1_loads = {}
    w2_loads = {}

    def load_w1(e, fg, buf):
        s, v = w1buf_free[buf]
        dstw = w1_sb[:, buf * KC * 512:(buf + 1) * KC * 512] \
            .rearrange("p (c f) -> p c f", f=512)
        if e in cast_done:
            sy.wait_ge(s.h, v)
            sy.wait_ge(CP.h, cast_done[e])
            srcw = w1b_d[e, :, fg * 512:(fg + 1) * 512] \
                .rearrange("(c p) f -> p c f", p=128)
            v = W1S[buf].inc(sy.dma_start(out=dstw, in_=srcw), 16)
            w1_loads[(e, fg)] = (W1S[buf], buf, v)
        else:
            gp.wait_ge(s.h, v)
            srcw = w1_d[e, :, fg * 512:(fg + 1) * 512] \
                .rearrange("(c p) f -> p c f", p=128)
            v = W1G[buf].inc(gp.dma_start(out=dstw, in_=srcw), 16)
            w1_loads[(e, fg)] = (W1G[buf], buf, v)

    def load_w2(e, dhl, fg, buf):
        s, v = w2buf_free[buf]
        dstw = w2_sb[:, buf * 4 * 512:(buf + 1) * 4 * 512] \
            .rearrange("p (c o) -> p c o", o=512)
        if e in cast_done:
            sy.wait_ge(s.h, v)
            sy.wait_ge(CP.h, cast_done[e])
            srcw = w2b_d[e, fg * 512:(fg + 1) * 512, dhl * 512:(dhl + 1) * 512] \
                .rearrange("(c p) o -> p c o", p=128)
            v = W2S[buf].inc(sy.dma_start(out=dstw, in_=srcw), 16)
            w2_loads[(e, dhl, fg)] = (W2S[buf], buf, v)
        else:
            gp.wait_ge(s.h, v)
            srcw = w2_d[e, fg * 512:(fg + 1) * 512, dhl * 512:(dhl + 1) * 512] \
                .rearrange("(c p) o -> p c o", p=128)
            v = W2G[buf].inc(gp.dma_start(out=dstw, in_=srcw), 16)
            w2_loads[(e, dhl, fg)] = (W2G[buf], buf, v)

    load_w1(0, 0, 0)

    oproj_ac = ACs.n
    op_banks = [("pb0", pb0[:, 0:512]), ("pb1", pb1[:, 0:512]),
                ("pbS0a", pbS0[:, 0:512]), ("pbS1a", pbS1[:, 0:512])]
    for dhl in range(2):
        pe.wait_ge(LDO.h, LDO.n)
        pe.wait_ge(LDW.h, LDW.n)
        if dhl == 0:
            pe.wait_ge(ACs.h, oproj_ac)
        for i in range(NT):
            bname, bank = op_banks[i % 4]
            pe_guard(bname)
            if bname in ("pbS0a", "pbS1a"):
                pe_guard(bname[:-1] + "b")
            for c in range(KC):
                pe.matmul(bank, o_all[:, c * T + i * 128:c * T + (i + 1) * 128],
                          woutT_sb[:, c * 512:(c + 1) * 512],
                          start=(c == 0), stop=False)
            last = pe.matmul(bank, ones_r[0:1, :],
                             bout_r[0:1, dhl * 512:(dhl + 1) * 512],
                             start=False, stop=True)
            pv = PEs.inc(last)
            dv.wait_ge(PEs.h, pv)
            ysl = y_sb[:, i * D + dhl * 512: i * D + (dhl + 1) * 512]
            i_ev = dv.scalar_tensor_tensor(out=ysl, in0=bank, scalar=1.0,
                                           in1=ysl, op0=ALU.mult, op1=ALU.add)
            ev = DVs.inc(i_ev)
            bank_free[bname] = (DVs, ev)
            if bname in ("pbS0a", "pbS1a"):
                bank_free[bname[:-1] + "b"] = (DVs, ev)
        if dhl == 0:
            sy.wait_ge(PEs.h, PEs.n)
            load_woutT(1)

    for g in range(5):
        bnameT = f"pT{g % 2}"
        bankT = pT0 if g % 2 == 0 else pT1
        pe_guard(bnameT)
        pe.wait_ge(GX.h, gather0_done)
        last = None
        for c in range(KC):
            last = pe.transpose(bankT[:, c * 128:(c + 1) * 128],
                                xe_tm[:, g * 1024 + c * 128:g * 1024 + (c + 1) * 128],
                                ident[:])
        pv = PEs.inc(last)
        dv.wait_ge(PEs.h, pv)
        dstx = bass.AP(xe_fm, g * 128, [[KC * 640, 128], [640, KC], [1, 128]])
        i_ev = dv.tensor_copy(dstx, bankT.ap().rearrange("p (c q) -> p c q", q=128))
        ev = DVs.inc(i_ev)
        bank_free[bnameT] = (DVs, ev)
    xe0_trs_pe = PEs.n
    xe0_fm_dv = DVs.n

    if debug:
        attnres_dv = DVs.n
        dbg_oa = nc.dram_tensor("dbg_oa", [128, NPAIR * T], F32, kind="ExternalOutput")
        sy.wait_ge(DVs.h, attnres_dv)
        gp.wait_ge(DVs.h, attnres_dv)
        LD.inc(gp.dma_start(out=dbg_oa[:, :], in_=o_all[:]), 16)
    if debug == 2:
        ST.inc(sy.dma_start(out=y_d[0:T, :].rearrange("(i p) d -> p i d", p=128),
                            in_=y_sb.ap().rearrange("p (i d) -> p i d", d=D)), 16)
        sy.wait_ge(ST.h, ST.n)
        sy.wait_ge(LX.h, LX.n)
        gp.wait_ge(LD.h, LD.n)
        gp.wait_ge(LDW.h, LDW.n)
        gp.wait_ge(LDO.h, LDO.n)
        es.close()
        return nc

    oproj_end_pe = PEs.n
    sy.wait_ge(DVs.h, DVs.n)
    ST.inc(sy.dma_start(out=y_d[0:T, :].rearrange("(i p) d -> p i d", p=128),
                        in_=y_sb.ap().rearrange("p (i d) -> p i d", d=D)), 16)
    base_store = ST.n

    w2buf_free = [(PEs, oproj_end_pe), (PEs, oproj_end_pe)]

    w1_order = [(e, fg) for e in range(NE) for fg in range(8)]
    w2_order = [(e, dhl, fg) for e in range(NE)
                for dhl in range(2) for fg in range(8)]
    w1_next = 2
    w2_next = 2
    load_w1(*w1_order[1], 1)
    load_w2(*w2_order[0], 0)
    load_w2(*w2_order[1], 1)

    gather_done = {}
    xe_trs_done = {}

    def gather_xe(e):
        dst = xe_tm
        gp.wait_ge(DVs.h, route2_dv)
        for k in range(4):
            gp.wait_ge(XNB4[k].h, XNB4[k].n)
        gp.wait_ge(PEs.h, xe_trs_done[e - 1])
        for g in range(NG):
            GX.inc(gp.indirect_dma_start(
                out=dst[:, g * 1024:(g + 1) * 1024],
                out_offset=None,
                in_=xnb_d.ap(),
                in_offset=IndirectOffsetOnAxis(
                    toks_u32[:, e * NG + g:e * NG + g + 1], 0)), 16)
        gather_done[e] = GX.n

    m1_banks = [("pbS0a", pbS0[:, 0:512]), ("pbS1a", pbS1[:, 0:512]),
                ("pbS0b", pbS0[:, 512:1024]), ("pbS1b", pbS1[:, 512:1024])]
    m2_banks = [("pb0", pb0[:, 0:512]), ("pb1", pb1[:, 0:512]),
                ("pbS0a", pbS0[:, 0:512]), ("pbS0b", pbS0[:, 512:1024]),
                ("pbS1a", pbS1[:, 0:512])]
    sct_expert_end = {}
    sct_half_prev = 0
    sct_hist = []

    def emit_xe_trans(e):
        srcb = xe_tm
        pe.wait_ge(GX.h, gather_done[e])
        for g in range(NG):
            bname = f"pT{g % 2}"
            bankT = pT0 if g % 2 == 0 else pT1
            pe_guard(bname)
            last = None
            for c in range(KC):
                last = pe.transpose(bankT[:, c * 128:(c + 1) * 128],
                                    srcb[:, g * 1024 + c * 128:g * 1024 + (c + 1) * 128],
                                    ident[:])
            pv = PEs.inc(last)
            dv.wait_ge(PEs.h, pv)
            dstx = bass.AP(xe_fm, g * 128, [[KC * 640, 128], [640, KC], [1, 128]])
            i_ev = dv.tensor_copy(dstx, bankT.ap().rearrange("p (c q) -> p c q", q=128))
            ev = DVs.inc(i_ev)
            bank_free[bname] = (DVs, ev)
        xe_trs_done[e] = PEs.n
        return DVs.n

    gather_done[0] = gather0_done
    xe_trs_done[0] = xe0_trs_pe
    xe_fm_ready = xe0_fm_dv
    gather_xe(1)

    for e in range(NE):
        first_m1 = True
        for fg in range(8):
            lsem, buf, ldv = w1_loads[(e, fg)]
            pe.wait_ge(lsem.h, ldv)
            if first_m1:
                pe.wait_ge(DVs.h, xe_fm_ready)
                first_m1 = False
            w1base = buf * KC * 512
            for fc in range(4):
                for blk, (b0, bw) in enumerate(((0, 512), (512, 128))):
                    bname, bank = m1_banks[(fc * 2 + blk) % 4]
                    pe_guard(bname)
                    last = None
                    for c in range(KC):
                        last = pe.matmul(
                            bank[:, 0:bw],
                            w1_sb[:, w1base + c * 512 + fc * 128:
                                  w1base + c * 512 + (fc + 1) * 128],
                            xe_fm[:, c * 640 + b0: c * 640 + b0 + bw],
                            start=(c == 0), stop=(c == KC - 1))
                    pv = PEs.inc(last)
                    ac.wait_ge(PEs.h, pv)
                    fi = fg * 4 + fc
                    i_ev = ac.activation(
                        out=h_sb[:, fi * 640 + b0: fi * 640 + b0 + bw],
                        in_=bank[:, 0:bw], func=AF.Relu)
                    av = ACs.inc(i_ev)
                    bank_free[bname] = (ACs, av)
            w1buf_free[buf] = (PEs, PEs.n)
            if w1_next < len(w1_order):
                load_w1(*w1_order[w1_next], w1_next % 2)
                w1_next += 1
        m1_ac = ACs.n
        if e + 1 < NE:
            xe_fm_ready = emit_xe_trans(e + 1)
            if e + 2 < NE:
                gather_xe(e + 2)
        first_m2 = True
        for dhl in range(2):
            for fg in range(8):
                lsem, buf, ldv = w2_loads[(e, dhl, fg)]
                pe.wait_ge(lsem.h, ldv)
                if first_m2:
                    pe.wait_ge(ACs.h, m1_ac)
                    first_m2 = False
                w2base = buf * 4 * 512
                last = None
                for gi_ in range(NG):
                    bname, bank = m2_banks[gi_]
                    if fg == 0:
                        pe_guard(bname)
                    for fc in range(4):
                        fi = fg * 4 + fc
                        last = pe.matmul(
                            bank,
                            h_sb[:, fi * 640 + gi_ * 128: fi * 640 + (gi_ + 1) * 128],
                            w2_sb[:, w2base + fc * 512:w2base + (fc + 1) * 512],
                            start=(fg == 0 and fc == 0),
                            stop=(fg == 7 and fc == 3))
                w2buf_free[buf] = (PEs, PEs.inc(last))
                if w2_next < len(w2_order):
                    load_w2(*w2_order[w2_next], w2_next % 2)
                    w2_next += 1
            m2_pe = PEs.n
            dv.wait_ge(PEs.h, m2_pe)
            dv.wait_ge(SCT.h, sct_half_prev)
            for gi_ in range(NG):
                bname, bank = m2_banks[gi_]
                slot = gi_
                i_ev = dv.tensor_scalar(
                    out=ysc[:, slot * 512:(slot + 1) * 512], in0=bank,
                    scalar1=gate_sl[:, e * NG + gi_:e * NG + gi_ + 1],
                    scalar2=None, op0=ALU.mult)
                ev = DVs.inc(i_ev)
                bank_free[bname] = (DVs, ev)
                gp.wait_ge(DVs.h, ev)
                if e > 0:
                    gp.wait_ge(SCT.h, sct_expert_end[e - 1])
                else:
                    gp.wait_ge(ST.h, base_store)
                i_sc = gp.indirect_dma_start(
                    out=y_d.ap(),
                    out_offset=IndirectOffsetOnAxis(
                        toks_u32[:, e * NG + gi_:e * NG + gi_ + 1], 0),
                    in_=ysc[:, slot * 512:(slot + 1) * 512],
                    in_offset=None,
                    element_offset=dhl * 512,
                    compute_op=ALU.add)
                sct_hist.append(SCT.inc(i_sc, 16))
            sct_half_prev = SCT.n
        sct_expert_end[e] = SCT.n

    gp.wait_ge(SCT.h, SCT.n)
    for b in range(2):
        gp.wait_ge(W1G[b].h, W1G[b].n)
        gp.wait_ge(W2G[b].h, W2G[b].n)
    sy.wait_ge(SCT.h, SCT.n)
    sy.wait_ge(ST.h, ST.n)
    sy.wait_ge(LX.h, LX.n)
    sy.wait_ge(LD.h, LD.n)
    gp.wait_ge(LDW.h, LDW.n)
    gp.wait_ge(LDP.h, LDP.n)
    sy.wait_ge(LDO.h, LDO.n)
    gp.wait_ge(CP.h, CP.n)
    gp.wait_ge(SCI.h, SCI.n)
    gp.wait_ge(SCR.h, SCR.n)
    gp.wait_ge(GX.h, GX.n)
    for k in range(4):
        sy.wait_ge(XNB4[k].h, XNB4[k].n)
    for s in (W1S[0], W1S[1], W2S[0], W2S[1]):
        sy.wait_ge(s.h, s.n)
    es.close()
    return nc


_NC_CACHE = None


def kernel(L0_attn_x, L0_attn_Wqkv, L0_attn_bqkv, L0_attn_Wout, L0_attn_bout,
           L0_ffn_Wr, L0_ffn_W1, L0_ffn_W2):
    global _NC_CACHE
    x = np.ascontiguousarray(np.asarray(L0_attn_x, dtype=np.float32))
    wqkvT = np.ascontiguousarray(np.asarray(L0_attn_Wqkv, dtype=np.float32).T)
    bqkv = np.ascontiguousarray(np.asarray(L0_attn_bqkv, dtype=np.float32))
    woutT = np.ascontiguousarray(np.asarray(L0_attn_Wout, dtype=np.float32).T)
    bout = np.ascontiguousarray(np.asarray(L0_attn_bout, dtype=np.float32))
    wr = np.ascontiguousarray(np.asarray(L0_ffn_Wr, dtype=np.float32))
    w1 = np.ascontiguousarray(np.asarray(L0_ffn_W1, dtype=np.float32))
    w2 = np.ascontiguousarray(np.asarray(L0_ffn_W2, dtype=np.float32))

    if _NC_CACHE is None:
        _NC_CACHE = build_nc()
    nc = _NC_CACHE

    ident_np = np.eye(128, dtype=ml_dtypes.bfloat16)
    ones_np = np.ones((1, 128), dtype=ml_dtypes.bfloat16)
    iota8_np = np.tile(np.arange(8, dtype=np.float32), (128, 1))
    bandn_np = np.full((128, 640), -30000.0, dtype=np.float32)
    for r in range(128):
        bandn_np[r, r:r + WIN] = 0.0
    bandn_np = bandn_np.astype(ml_dtypes.bfloat16)
    tri_np = np.zeros((128, 128), dtype=ml_dtypes.bfloat16)
    for r in range(128):
        tri_np[r, r + 1:] = 1.0
    ones128_np = np.ones((128, 128), dtype=ml_dtypes.bfloat16)
    tokid_np = (np.arange(NT, dtype=np.float32)[None, :] * 128
                + np.arange(128, dtype=np.float32)[:, None])

    in_maps = []
    for b in range(8):
        in_maps.append({
            "x": x[b], "wqkvT": wqkvT, "bqkv": bqkv, "woutT": woutT,
            "bout": bout, "wr": wr, "w1": w1, "w2": w2,
            "ident_c": ident_np, "ones_c": ones_np, "iota8_c": iota8_np,
            "bandn_c": bandn_np, "tri_c": tri_np, "ones128_c": ones128_np,
            "tokid_c": tokid_np, "ident32_c": np.eye(128, dtype=np.float32),
        })
    res = run_bass_kernel_spmd(nc, in_maps, core_ids=list(range(8)))
    out = np.stack([res.results[b]["y"][:T] for b in range(8)], axis=0)
    return out


# version 19 (final)
# speedup vs baseline: 1.1061x; 1.0369x over previous
import numpy as np
import ml_dtypes
from contextlib import ExitStack

import concourse.bass as bass
import concourse.mybir as mybir
from concourse.bass_utils import run_bass_kernel_spmd

F32 = mybir.dt.float32
BF16 = mybir.dt.bfloat16
U32 = mybir.dt.uint32
AF = mybir.ActivationFunctionType
ALU = mybir.AluOpType
AX = mybir.AxisListType

T = 2048
D = 1024
NT = T // 128
KC = D // 128
NPAIR = 8
WIN = 512
FF = 4096
NE = 8


class Sem:
    def __init__(self, es, nc, name):
        self.h = es.enter_context(nc.semaphore(name))
        self.n = 0

    def inc(self, inst, k=1):
        self.n += k
        inst.then_inc(self.h, k)
        return self.n


def build_nc(debug=0):
    nc = bass.Bass("TRN2", target_bir_lowering=False,
                   detect_race_conditions=True)
    es = ExitStack()

    x_d = nc.dram_tensor("x", [T, D], F32, kind="ExternalInput")
    wqkvT_d = nc.dram_tensor("wqkvT", [D, 3 * D], F32, kind="ExternalInput")
    bqkv_d = nc.dram_tensor("bqkv", [3 * D], F32, kind="ExternalInput")
    woutT_d = nc.dram_tensor("woutT", [D, D], F32, kind="ExternalInput")
    bout_d = nc.dram_tensor("bout", [D], F32, kind="ExternalInput")
    wr_d = nc.dram_tensor("wr", [D, NE], F32, kind="ExternalInput")
    w1_d = nc.dram_tensor("w1", [NE, D, FF], F32, kind="ExternalInput")
    w2_d = nc.dram_tensor("w2", [NE, FF, D], F32, kind="ExternalInput")
    y_d = nc.dram_tensor("y", [T + 8, D], F32, kind="ExternalOutput")
    CAP = 640
    NG = CAP // 128
    xnb_d = nc.dram_tensor("xnb", [T + 8, D], BF16, kind="Internal")
    idxg_d = nc.dram_tensor("idxg", [NE * CAP + 128, 2], F32, kind="Internal")
    w1b_d = nc.dram_tensor("w1b", [NE, D, FF], BF16, kind="Internal")
    w2b_d = nc.dram_tensor("w2b", [NE, FF, D], BF16, kind="Internal")
    wqkvb_d = nc.dram_tensor("wqkvb", [D, 3 * D], BF16, kind="Internal")
    woutb_d = nc.dram_tensor("woutb", [D, D], BF16, kind="Internal")
    ident_c = nc.dram_tensor("ident_c", [128, 128], BF16, kind="ExternalInput")
    ones_c = nc.dram_tensor("ones_c", [1, 128], BF16, kind="ExternalInput")
    iota8_c = nc.dram_tensor("iota8_c", [128, 8], F32, kind="ExternalInput")
    bandn_c = nc.dram_tensor("bandn_c", [128, 640], BF16, kind="ExternalInput")
    tri_c = nc.dram_tensor("tri_c", [128, 128], BF16, kind="ExternalInput")
    ones128_c = nc.dram_tensor("ones128_c", [128, 128], BF16, kind="ExternalInput")
    tokid_c = nc.dram_tensor("tokid_c", [128, NT], F32, kind="ExternalInput")
    ident32_c = nc.dram_tensor("ident32_c", [128, 128], F32, kind="ExternalInput")

    SB = nc.alloc_sbuf_tensor
    y_sb = SB("y_sb", [128, NT * D], F32)
    xn_fm = SB("xn_fm", [128, KC * T], BF16)
    ident = SB("ident", [128, 128], BF16)
    ones_r = SB("ones_r", [1, 128], BF16)
    iota8 = SB("iota8", [128, 8], F32)
    bandn = SB("bandn", [128, 640], BF16)
    bq_sb = SB("bq_sb", [128, 24], F32)
    bq8_sb = SB("bq8_sb", [128, 8], F32)
    bout_r = SB("bout_r", [1, D], BF16)
    wr_sb = SB("wr_sb", [128, KC * NE], F32)
    ident32 = SB("ident32", [128, 128], F32)
    logit_sb = SB("logit_sb", [128, NT * 8], F32)
    zrow = SB("zrow", [1, D], BF16)
    gsel = SB("gsel", [128, 2 * NT], F32)
    probs = SB("probs", [128, NT * 8], F32)
    gates = SB("gates", [128, NT * 8], F32)
    s_mean = SB("s_mean", [128, NT], F32)
    s_aux = SB("s_aux", [128, 24], F32)
    s_rstd = SB("s_rstd", [128, NT], F32)
    s_ssq = SB("s_ssq", [128, NT], F32)
    s_m2 = SB("s_m2", [128, NT], F32)
    s_nmr = SB("s_nmr", [128, NT], F32)
    s_nmx = SB("s_nmx", [128, NT], F32)
    s_sme = SB("s_sme", [128, NT], F32)
    m8a = SB("m8a", [128, NT * 8], F32)
    i8a = SB("i8a", [128, NT * 8], U32)
    i8fa = SB("i8fa", [128, NT * 8], F32)
    mska = SB("mska", [128, NT * 8], F32)
    mskb = SB("mskb", [128, NT * 8], F32)
    mskc = SB("mskc", [128, NT * 8], F32)
    tri_sb = SB("tri_sb", [128, 128], BF16)
    ones128 = SB("ones128", [128, 128], BF16)
    tokid = SB("tokid", [128, NT], F32)
    flag_bf = SB("flag_bf", [128, NT * 8], BF16)
    pos_tm = SB("pos_tm", [128, NT * 8], F32)
    possel = SB("possel", [128, 4 * NT], F32)
    scat_src = SB("scat_src", [128, NT * 4], F32)
    dest_i = SB("dest_i", [128, NT * 2], U32)
    idx_all = SB("idx_all", [128, 8 * 5 * 2], F32)
    toks_u32 = SB("toks_u32", [128, 8 * 5], U32)
    gate_sl = SB("gate_sl", [128, 8 * 5], F32)
    zfill = SB("zfill", [128, 8 * 5 * 2], F32)

    ARENA = 98304
    arena = SB("arena", [128, ARENA], mybir.dt.uint8)
    abase = nc.lookup_mloc(arena).addr

    def AT(name, shape, dtype, off):
        return nc.alloc_sbuf_tensor_at(name, shape, dtype, offset=abase + off)

    o_all = AT("o_all", [128, NPAIR * T], BF16, 0)
    q_sb = AT("q_sb", [128, T], BF16, 32768)
    k_pad = AT("k_pad", [128, T + 512], BF16, 36864)
    v_st = AT("v_st", [128, T], BF16, 41984)
    v_tm = AT("v_tm", [128, T + 512], BF16, 46080)
    attnT0 = AT("attnT0", [128, 8 * 512], BF16, 51200)
    attnT1 = AT("attnT1", [128, 8 * 512], BF16, 59392)
    p_f32 = AT("p_f32", [128, 4 * 640], BF16, 67584)
    a_bf = AT("a_bf", [128, 4 * 640], BF16, 72704)
    wq_sb = AT("wq_sb", [128, KC * 128], BF16, 77824)
    wk_sb = AT("wk_sb", [128, KC * 128], BF16, 79872)
    wv_sb = AT("wv_sb", [128, KC * 128], BF16, 81920)
    wq2_sb = AT("wq2_sb", [128, KC * 128], BF16, 92160)
    wk2_sb = AT("wk2_sb", [128, KC * 128], BF16, 94208)
    wv2_sb = AT("wv2_sb", [128, KC * 128], BF16, 96256)
    woutT_sb = AT("woutT_sb", [128, KC * 512], BF16, 83968)
    junk_sb = AT("junk_sb", [128, D], BF16, 0)
    xn_sts = [AT(f"xn_st{k}", [128, D], F32, 2048 + 4096 * k) for k in range(4)]
    xnb_sts = [AT(f"xnb_st{k}", [128, D], BF16, 18432 + 2048 * k) for k in range(4)]
    xn32b = AT("xn32b", [128, 2 * D], F32, 26624)
    h_sb = AT("h_sb", [128, 32 * 640], BF16, 0)
    xe_tm = AT("xe_tm", [128, 5 * 1024], BF16, 40960)
    xe_fm = AT("xe_fm", [128, KC * 640], BF16, 51200)
    woutT1_sb = AT("woutT1_sb", [128, KC * 512], BF16, 51200)
    w1_sb = AT("w1_sb", [128, 2 * KC * 512], BF16, 61440)
    w2_sb = AT("w2_sb", [128, 2 * 4 * 512], BF16, 77824)
    ysc = AT("ysc", [128, 5 * 512], F32, 86016)

    PS = nc.alloc_psum_tensor
    pb0 = PS("pb0", [128, 512], F32)
    pb1 = PS("pb1", [128, 512], F32)
    pbS0 = PS("pbS0", [128, 1024], F32)
    pbS1 = PS("pbS1", [128, 1024], F32)
    pT0 = PS("pT0", [128, 1024], BF16)
    pT1 = PS("pT1", [128, 1024], BF16)

    LD = Sem(es, nc, "LD")
    LDW = Sem(es, nc, "LDW")
    CP = Sem(es, nc, "CP")
    LDO = Sem(es, nc, "LDO")
    LDP = Sem(es, nc, "LDP")
    W1S = [Sem(es, nc, "W1A"), Sem(es, nc, "W1B")]
    W2S = [Sem(es, nc, "W2A"), Sem(es, nc, "W2B")]
    W1G = [Sem(es, nc, "W1GA"), Sem(es, nc, "W1GB")]
    W2G = [Sem(es, nc, "W2GA"), Sem(es, nc, "W2GB")]
    LX = Sem(es, nc, "LX")
    ST = Sem(es, nc, "ST")
    XNB4 = [Sem(es, nc, f"XNB{k}") for k in range(4)]
    XQ = [Sem(es, nc, f"XQ{k}") for k in range(4)]
    SCI = Sem(es, nc, "SCI")
    SCR = Sem(es, nc, "SCR")
    GX = Sem(es, nc, "GX")
    STH = [Sem(es, nc, "STH0"), Sem(es, nc, "STH1")]
    SCT = Sem(es, nc, "SCT")
    PEs = Sem(es, nc, "PEs")
    ACs = Sem(es, nc, "ACs")
    DVs = Sem(es, nc, "DVs")
    ALL_SEMS = [LD, LDW, LDO, LDP, CP, W1S[0], W1S[1], W2S[0], W2S[1],
                W1G[0], W1G[1], W2G[0], W2G[1], STH[0], STH[1],
                LX, ST, SCI, SCR, GX, SCT, PEs, ACs, DVs] + XNB4 + XQ

    pe, ac, dv, sy, gp = nc.tensor, nc.scalar, nc.vector, nc.sync, nc.gpsimd

    for s in ALL_SEMS:
        sy.sem_clear(s.h)
    nc.all_engine_barrier()

    bank_free = {}

    def pe_guard(name):
        if name in bank_free:
            s, v = bank_free[name]
            pe.wait_ge(s.h, v)

    LX.inc(sy.dma_start(out=ident[:], in_=ident_c[:, :]), 16)
    LX.inc(sy.dma_start(out=ones_r[0:1, :], in_=ones_c[:, :]), 16)
    LX.inc(sy.dma_start(out=iota8[:], in_=iota8_c[:, :]), 16)
    LX.inc(sy.dma_start(out=bandn[:], in_=bandn_c[:, :]), 16)
    LX.inc(sy.dma_start(out=bq_sb[:], in_=bqkv_d.ap().rearrange("(c p) -> p c", p=128),
                        allow_slow_non_contiguous=True), 16)
    LDW.inc(gp.dma_start(out=bout_r[0:1, :], in_=bout_d.ap().rearrange("(o n) -> o n", o=1)), 16)
    LX.inc(sy.dma_start(out=tri_sb[:], in_=tri_c[:, :]), 16)
    LX.inc(sy.dma_start(out=ones128[:], in_=ones128_c[:, :]), 16)
    LX.inc(sy.dma_start(out=tokid[:], in_=tokid_c[:, :]), 16)
    LX.inc(sy.dma_start(out=ident32[:], in_=ident32_c[:, :]), 16)
    LX.inc(sy.dma_start(out=wr_sb.ap().rearrange("p (c e) -> p c e", e=8),
                        in_=wr_d.ap().rearrange("(c p) e -> p c e", p=128),
                        allow_slow_non_contiguous=True), 16)
    lx_all = LX.n
    y_sb3 = y_sb.ap().rearrange("p (i d) -> p i d", d=D)
    x3 = x_d.ap().rearrange("(i p) d -> p i d", p=128)
    for q in range(4):
        eng = sy if q % 2 == 0 else ac
        XQ[q].inc(eng.dma_start(out=y_sb3[:, q * 4:(q + 1) * 4, :],
                                in_=x3[:, q * 4:(q + 1) * 4, :]), 16)

    for hp0 in (0, 1):
        dsts = (wq_sb, wk_sb, wv_sb) if hp0 == 0 else (wq2_sb, wk2_sb, wv2_sb)
        for (wsb, col0) in zip(dsts, (hp0 * 128, D + hp0 * 128, 2 * D + hp0 * 128)):
            gp.wait_ge(LDP.h, LDP.n)
            srcw = wqkvT_d[:, col0:col0 + 128].rearrange("(c p) o -> p c o", p=128)
            LDP.inc(gp.dma_start(out=wsb.ap().rearrange("p (c o) -> p c o", o=128),
                                 in_=srcw), 16)
    ld_pair01 = LDP.n
    cast_done = {}
    CP.inc(gp.dma_start(out=wqkvb_d[:, :], in_=wqkvT_d[:, :]), 16)
    cast_qkv = CP.n
    gp.wait_ge(CP.h, CP.n)
    CP.inc(gp.dma_start(out=woutb_d[:, :], in_=woutT_d[:, :]), 16)
    cast_wout = CP.n

    pe.wait_ge(LX.h, lx_all)

    ys3 = y_sb.ap().rearrange("p (i d) -> p i d", d=D)
    sq_ac = {}
    nmr_dv = {}
    gates_done = []
    ln_ac_hist = []
    trs_pv1 = []
    xn_done_dv = []
    router_pe = {}

    sq_prev = [None]

    def emit_stats(q):
        sl = slice(q * 4, q * 4 + 4)
        ac.wait_ge(XQ[q].h, 16)
        for i in range(q * 4, q * 4 + 4):
            if sq_prev[0] is not None:
                ac.wait_ge(ACs.h, sq_prev[0])
            i_sq = ac.activation(out=junk_sb[:], in_=y_sb[:, i * D:(i + 1) * D],
                                 func=AF.Square, accum_out=s_ssq[:, i:i + 1])
            sq_prev[0] = ACs.inc(i_sq)
        sq_ac[q] = sq_prev[0]
        dv.wait_ge(XQ[q].h, 16)
        dv.reduce_sum(s_mean[:, sl], ys3[:, sl, :], axis=AX.X)
        dv.drain()
        dv.tensor_scalar(out=s_mean[:, sl], in0=s_mean[:, sl], scalar1=1.0 / D,
                         scalar2=None, op0=ALU.mult)
        dv.drain()
        dv.tensor_mul(s_m2[:, sl], s_mean[:, sl], s_mean[:, sl])
        dv.drain()
        dv.tensor_scalar(out=s_m2[:, sl], in0=s_m2[:, sl], scalar1=1.0e-5,
                         scalar2=None, op0=ALU.subtract)
        dv.wait_ge(ACs.h, sq_ac[q])
        dv.drain()
        i_t = dv.scalar_tensor_tensor(out=s_rstd[:, sl], in0=s_ssq[:, sl],
                                      scalar=1.0 / D, in1=s_m2[:, sl],
                                      op0=ALU.mult, op1=ALU.subtract)
        DVs.inc(i_t)
        ac.wait_ge(DVs.h, DVs.n)
        i_r = ac.activation(out=s_rstd[:, sl], in_=s_rstd[:, sl], func=AF.Sqrt)
        ACs.inc(i_r)
        dv.wait_ge(ACs.h, ACs.n)
        dv.reciprocal(s_rstd[:, sl], s_rstd[:, sl])
        dv.drain()
        i_n = dv.scalar_tensor_tensor(out=s_nmr[:, sl], in0=s_mean[:, sl],
                                      scalar=-1.0, in1=s_rstd[:, sl],
                                      op0=ALU.mult, op1=ALU.mult)
        nmr_dv[q] = DVs.inc(i_n)

    def emit_router(i):
        base = (i % 2) * D
        pe.wait_ge(DVs.h, xn_done_dv[i])
        last = None
        for c in range(KC):
            last = pe.matmul(pbS0[:, i * 8:(i + 1) * 8],
                             xn32b[:, base + c * 128: base + (c + 1) * 128],
                             wr_sb[:, c * 8:(c + 1) * 8],
                             start=(c == 0), stop=(c == KC - 1))
        router_pe[i] = PEs.inc(last)

    def emit_tile(i):
        s4 = i % 4
        xn_st = xn_sts[s4]
        xnb_st = xnb_sts[s4]
        ys = y_sb[:, i * D:(i + 1) * D]
        if s4 == 0:
            ac.wait_ge(DVs.h, nmr_dv[i // 4])
        if i >= 4:
            ac.wait_ge(PEs.h, trs_pv1[i - 4])
            ac.wait_ge(XNB4[s4].h, XNB4[s4].n)
        i_xn = ac.activation(out=xn_st[:], in_=ys, func=AF.Identity,
                             scale=s_rstd[:, i:i + 1], bias=s_nmr[:, i:i + 1])
        i_xnb = ac.activation(out=xnb_st[:], in_=ys, func=AF.Identity,
                              scale=s_rstd[:, i:i + 1], bias=s_nmr[:, i:i + 1])
        ln_ac_hist.append(ACs.inc(i_xnb))
        sy.wait_ge(ACs.h, ln_ac_hist[i])
        XNB4[s4].inc(sy.dma_start(out=xnb_d[i * 128:(i + 1) * 128, :],
                                  in_=xnb_st[:]), 16)
        base = (i % 2) * D
        pe.wait_ge(ACs.h, ln_ac_hist[i])
        pvs = []
        for ghalf, bname in ((0, "pbS1a"), (1, "pbS1b")):
            bslice = pbS1[:, 0:512] if ghalf == 0 else pbS1[:, 512:1024]
            pe_guard(bname)
            last = None
            for cc in range(4):
                c = ghalf * 4 + cc
                last = pe.transpose(bslice[:, cc * 128:(cc + 1) * 128],
                                    xn_st[:, c * 128:(c + 1) * 128], ident32[:])
            pvs.append(PEs.inc(last))
        trs_pv1.append(PEs.n)
        if i > 0:
            emit_router(i - 1)
        for ghalf, bname in ((0, "pbS1a"), (1, "pbS1b")):
            bslice = pbS1[:, 0:512] if ghalf == 0 else pbS1[:, 512:1024]
            dv.wait_ge(PEs.h, pvs[ghalf])
            dst = bass.AP(xn_fm, ghalf * 4 * T + i * 128,
                          [[KC * T, 128], [T, 4], [1, 128]])
            dv.tensor_copy(dst, bslice.rearrange("p (c q) -> p c q", q=128))
            i_ev = dv.tensor_copy(
                xn32b[:, base + ghalf * 512: base + (ghalf + 1) * 512], bslice)
            ev = DVs.inc(i_ev)
            bank_free[bname] = (DVs, ev)
        xn_done_dv.append(DVs.n)

    emit_stats(0)
    for q in range(4):
        if q + 1 < 4:
            emit_stats(q + 1)
        for i in range(q * 4, q * 4 + 4):
            emit_tile(i)
    emit_router(NT - 1)
    xn_all_dv = max(xn_done_dv)

    dv.wait_ge(LX.h, lx_all)
    i_b = dv.tensor_scalar(out=bq8_sb[:], in0=bq_sb[:, 0:8], scalar1=0.125,
                           scalar2=None, op0=ALU.mult)
    bq8_dv = DVs.inc(i_b)
    dv.wait_ge(PEs.h, router_pe[NT - 1])
    i_lc = dv.tensor_copy(logit_sb[:], pbS0[:, 0:NT * 8])
    lc_dv = DVs.inc(i_lc)
    bank_free["pbS0a"] = (DVs, lc_dv)
    dv.drain()

    i_m = dv.reduce_max(s_nmx[:], logit_sb.ap().rearrange("p (i e) -> p i e", e=8),
                        axis=AX.X, negate=True)
    DVs.inc(i_m)
    ac.wait_ge(DVs.h, DVs.n)
    last_e = None
    for i in range(NT):
        last_e = ac.activation(out=probs[:, i * 8:(i + 1) * 8],
                               in_=logit_sb[:, i * 8:(i + 1) * 8], func=AF.Exp,
                               bias=s_nmx[:, i:i + 1],
                               accum_out=s_sme[:, i:i + 1])
    ACs.inc(last_e)
    dv.wait_ge(ACs.h, ACs.n)
    dv.reciprocal(s_sme[:], s_sme[:])
    dv.drain()
    for i in range(NT):
        dv.tensor_scalar(out=probs[:, i * 8:(i + 1) * 8],
                         in0=probs[:, i * 8:(i + 1) * 8],
                         scalar1=s_sme[:, i:i + 1], scalar2=None, op0=ALU.mult)
    dv.drain()
    for i in range(NT):
        dv.max(out=m8a[:, i * 8:(i + 1) * 8], in_=probs[:, i * 8:(i + 1) * 8])
    dv.drain()
    for i in range(NT):
        dv.max_index(i8a[:, i * 8:(i + 1) * 8], m8a[:, i * 8:(i + 1) * 8],
                     probs[:, i * 8:(i + 1) * 8])
    dv.drain()
    dv.tensor_copy(i8fa[:], i8a[:])
    dv.drain()
    for i in range(NT):
        dv.tensor_scalar(out=mska[:, i * 8:(i + 1) * 8], in0=iota8[:],
                         scalar1=i8fa[:, i * 8:i * 8 + 1], scalar2=None,
                         op0=ALU.is_equal)
        dv.tensor_scalar(out=mskb[:, i * 8:(i + 1) * 8], in0=iota8[:],
                         scalar1=i8fa[:, i * 8 + 1:i * 8 + 2], scalar2=None,
                         op0=ALU.is_equal)
    dv.drain()
    dv.tensor_add(mskc[:], mska[:], mskb[:])
    dv.drain()
    i_g = dv.tensor_mul(gates[:], mskc[:], probs[:])
    gv = DVs.inc(i_g)
    gates_done = [gv] * NT
    dv.drain()
    dv.tensor_mul(mskc[:], mska[:], probs[:])
    dv.drain()
    dv.reduce_sum(gsel.ap().rearrange("p (k i) -> p k i", k=2)[:, 0:1, :],
                  mskc.ap().rearrange("p (i e) -> p i e", e=8), axis=AX.X)
    dv.drain()
    dv.tensor_mul(mskc[:], mskb[:], probs[:])
    dv.drain()
    dv.reduce_sum(gsel.ap().rearrange("p (k i) -> p k i", k=2)[:, 1:2, :],
                  mskc.ap().rearrange("p (i e) -> p i e", e=8), axis=AX.X)
    dv.drain()
    i_f = dv.tensor_scalar(out=flag_bf[:], in0=gates[:], scalar1=0.0, scalar2=None,
                           op0=ALU.is_gt)
    flag_dv = DVs.inc(i_f)

    xn_all_dv = max(xn_done_dv)

    if debug:
        dbg_xn = nc.dram_tensor("dbg_xn", [128, KC * T], F32, kind="ExternalOutput")
        dbg_pr = nc.dram_tensor("dbg_pr", [128, NT * 8], F32, kind="ExternalOutput")
        dbg_ga = nc.dram_tensor("dbg_ga", [128, NT * 8], F32, kind="ExternalOutput")
        sy.wait_ge(DVs.h, DVs.n)
        gp.wait_ge(DVs.h, DVs.n)
        LD.inc(gp.dma_start(out=dbg_xn[:, :], in_=xn_fm[:]), 16)
        ST.inc(sy.dma_start(out=dbg_pr[:, :], in_=probs[:]), 16)
        ST.inc(sy.dma_start(out=dbg_ga[:, :], in_=gates[:]), 16)
    if debug == 1:
        ST.inc(sy.dma_start(out=y_d[0:T, :].rearrange("(i p) d -> p i d", p=128),
                            in_=y_sb.ap().rearrange("p (i d) -> p i d", d=D)), 16)
        sy.wait_ge(ST.h, ST.n)
        sy.wait_ge(LX.h, LX.n)
        gp.wait_ge(LD.h, LD.n)
        gp.wait_ge(LDW.h, LDW.n)
        gp.wait_ge(CP.h, CP.n)
        es.close()
        return nc

    from concourse.bass_core_rust import IndirectOffsetOnAxis

    def emit_compaction():
        pe.wait_ge(DVs.h, flag_dv)
        pe_guard("pb0")
        last = None
        for i in range(NT):
            for c in range(i + 1):
                last = pe.matmul(pb0[:, i * 8:(i + 1) * 8],
                                 tri_sb[:] if c == i else ones128[:],
                                 flag_bf[:, c * 8:(c + 1) * 8],
                                 start=(i == 0 and c == 0),
                                 stop=(i == NT - 1 and c == i))
        scan_pe = PEs.inc(last)
        dv.wait_ge(PEs.h, scan_pe)
        i_ev = dv.tensor_copy(pos_tm[:], pb0[:, 0:NT * 8])
        scan_dv = DVs.inc(i_ev)
        bank_free["pb0"] = (DVs, scan_dv)
        dv.drain()
        for i in range(NT):
            p8 = pos_tm[:, i * 8:(i + 1) * 8]
            dv.tensor_mul(mska[:, i * 8:(i + 1) * 8], mska[:, i * 8:(i + 1) * 8], p8)
            dv.tensor_mul(mskb[:, i * 8:(i + 1) * 8], mskb[:, i * 8:(i + 1) * 8], p8)
        dv.drain()
        pos3 = possel.ap().rearrange("p (k i) -> p k i", k=4)
        dv.reduce_sum(pos3[:, 0:1, :], mska.ap().rearrange("p (i e) -> p i e", e=8),
                      axis=AX.X, apply_transpose=False)
        dv.reduce_sum(pos3[:, 1:2, :], mskb.ap().rearrange("p (i e) -> p i e", e=8),
                      axis=AX.X, apply_transpose=False)
        dv.drain()
        ea = mskc[:, 0:2 * NT]
        i8v = i8fa.ap().rearrange("p (i e) -> p i e", e=8)
        for ch in range(2):
            dv.tensor_copy(ea[:, ch * NT:(ch + 1) * NT], i8v[:, :, ch:ch + 1])
        dv.drain()
        pos2 = possel[:, 0:2 * NT]
        dst2 = possel[:, 2 * NT:4 * NT]
        dv.scalar_tensor_tensor(out=dst2, in0=ea, scalar=float(CAP), in1=pos2,
                                op0=ALU.mult, op1=ALU.add)
        dv.drain()
        dv.tensor_scalar(out=pos2, in0=pos2, scalar1=float(CAP), scalar2=None,
                         op0=ALU.is_lt)
        dv.drain()
        dv.scalar_tensor_tensor(out=dst2, in0=dst2, scalar=1.0, in1=pos2,
                                op0=ALU.mult, op1=ALU.mult)
        dv.drain()
        dv.scalar_tensor_tensor(out=pos2, in0=pos2, scalar=-float(NE * CAP),
                                in1=dst2, op0=ALU.mult, op1=ALU.add)
        dv.drain()
        dv.tensor_scalar(out=dst2, in0=pos2, scalar1=float(NE * CAP),
                         scalar2=None, op0=ALU.add)
        for ch in range(2):
            dv.tensor_copy(bass.AP(scat_src, 2 * ch, [[64, 128], [4, NT], [1, 1]]),
                           tokid.ap().rearrange("p (i o) -> p i o", o=1))
            dv.tensor_copy(bass.AP(scat_src, 2 * ch + 1, [[64, 128], [4, NT], [1, 1]]),
                           gsel[:, ch * NT:(ch + 1) * NT].rearrange("p (i o) -> p i o", o=1))
        dv.drain()
        dv.tensor_copy(dest_i[:], possel[:, 2 * NT:4 * NT])
        dv.drain()
        zf3 = zfill.ap().rearrange("p (e g b) -> p e g b", b=2, g=NG)
        for g in range(NG):
            dv.memset(zf3[:, :, g:g + 1, 0:1], float(T + g))
        dv.memset(zf3[:, :, :, 1:2], 0.0)
        i_zr = dv.memset(zrow[0:1, :], 0.0)
        route_dv = DVs.inc(i_zr)

        gp.wait_ge(DVs.h, route_dv)
        SCI.inc(gp.dma_start(out=idxg_d[0:NE * CAP, :].rearrange("(e g p) b -> p e g b", p=128, g=NG),
                             in_=zfill.ap().rearrange("p (e g b) -> p e g b", b=2, g=NG)), 16)
        for tg in range(8):
            SCI.inc(gp.dma_start(out=xnb_d[T + tg:T + tg + 1, :], in_=zrow[0:1, :]), 16)
        gp.wait_ge(SCI.h, SCI.n)
        idxg_flat = idxg_d.ap()
        for i in range(NT):
            for ch in range(2):
                col = (i * 2 + ch) * 2
                SCI.inc(gp.indirect_dma_start(
                    out=idxg_flat,
                    out_offset=IndirectOffsetOnAxis(dest_i[:, NT * ch + i:NT * ch + i + 1], 0),
                    in_=scat_src[:, col:col + 2], in_offset=None), 16)
        gp.wait_ge(SCI.h, SCI.n)
        SCR.inc(gp.dma_start(
            out=idx_all.ap().rearrange("p (e g b) -> p e g b", b=2, g=NG),
            in_=idxg_d[0:NE * CAP, :].rearrange("(e g p) b -> p e g b", p=128, g=NG)), 16)

    def emit_route2():
        dv.wait_ge(SCR.h, SCR.n)
        ia3 = idx_all.ap().rearrange("p (s b) -> p s b", b=2)
        dv.tensor_copy(toks_u32[:], ia3[:, :, 0:1])
        i_gs = dv.tensor_copy(gate_sl[:], ia3[:, :, 1:2])
        return DVs.inc(i_gs)

    def pair_bufs(hp):
        return (wq_sb, wk_sb, wv_sb) if hp % 2 == 0 else (wq2_sb, wk2_sb, wv2_sb)

    def load_pair_w(hp):
        sy.wait_ge(CP.h, cast_qkv)
        for (wsb, col0) in zip(pair_bufs(hp), (hp * 128, D + hp * 128,
                                               2 * D + hp * 128)):
            src = wqkvb_d[:, col0:col0 + 128].rearrange("(c p) o -> p c o", p=128)
            LD.inc(sy.dma_start(out=wsb.ap().rearrange("p (c o) -> p c o", o=128),
                                in_=src), 16)
        return LD.n

    def load_woutT(dhl, dst_sb):
        sy.wait_ge(CP.h, cast_wout)
        for half in range(2):
            srcw = woutb_d[half * 512:(half + 1) * 512,
                           dhl * 512:(dhl + 1) * 512] \
                .rearrange("(c p) o -> p c o", p=128)
            dstw = dst_sb[:, half * 4 * 512:(half + 1) * 4 * 512] \
                .rearrange("p (c o) -> p c o", o=512)
            LDO.inc(sy.dma_start(out=dstw, in_=srcw), 16)
        return LDO.n

    dv.memset(k_pad[:, 0:256], 0.0)
    dv.memset(k_pad[:, T + 256:T + 512], 0.0)
    dv.memset(v_tm[:, 0:256], 0.0)
    i_pad = dv.memset(v_tm[:, T + 256:T + 512], 0.0)
    pads_dv = DVs.inc(i_pad)

    last_av_pe = 0
    abf_slot_prev = {}
    ldw_for = {0: ld_pair01, 1: ld_pair01}

    for k in range(4):
        ac.wait_ge(XNB4[k].h, XNB4[k].n)
    for hp in range(NPAIR):
        if hp < 2:
            pe.wait_ge(LDP.h, ld_pair01)
        else:
            pe.wait_ge(LD.h, LD.n)
        if hp == 0:
            pe.wait_ge(DVs.h, xn_all_dv)
            ac.wait_ge(DVs.h, bq8_dv)
        wqp, wkp, wvp = pair_bufs(hp)
        qkv_sets = ((wvp, v_st, 16 + hp, 0), (wqp, q_sb, hp, 0),
                    (wkp, k_pad, 8 + hp, 256))
        for (wsb, dst, bidx, off) in qkv_sets:
            for tb in range(4):
                bname = f"pb{tb % 2}"
                bank = pb0 if tb % 2 == 0 else pb1
                pe_guard(bname)
                last = None
                for c in range(KC):
                    last = pe.matmul(bank[:], wsb[:, c * 128:(c + 1) * 128],
                                     xn_fm[:, c * T + tb * 512: c * T + (tb + 1) * 512],
                                     start=(c == 0), stop=(c == KC - 1))
                pv = PEs.inc(last)
                ac.wait_ge(PEs.h, pv)
                d0 = tb * 512 + off
                if wsb is wqp:
                    i_ev = ac.activation(out=dst[:, d0:d0 + 512], in_=bank[:],
                                         func=AF.Identity, scale=0.125,
                                         bias=bq8_sb[:, hp:hp + 1])
                else:
                    i_ev = ac.activation(out=dst[:, d0:d0 + 512], in_=bank[:],
                                         func=AF.Identity,
                                         bias=bq_sb[:, bidx:bidx + 1])
                av = ACs.inc(i_ev)
                bank_free[bname] = (ACs, av)
            if wsb is wvp:
                v_ac = ACs.n
        qkv_ac = ACs.n
        pe.wait_ge(ACs.h, v_ac)
        if hp == 0:
            pe.wait_ge(DVs.h, pads_dv)
        for g in range(2):
            bname = f"pT{g}"
            bankT = pT0 if g == 0 else pT1
            pe_guard(bname)
            last = None
            for j in range(8):
                jj = g * 8 + j
                last = pe.transpose(bankT[:, j * 128:(j + 1) * 128],
                                    v_st[:, jj * 128:(jj + 1) * 128], ident[:])
            pv = PEs.inc(last)
            dv.wait_ge(PEs.h, pv)
            i_ev = dv.tensor_copy(v_tm[:, 256 + g * 1024: 256 + (g + 1) * 1024],
                                  bankT[:])
            ev = DVs.inc(i_ev)
            bank_free[bname] = (DVs, ev)

        if hp + 2 < NPAIR:
            sy.wait_ge(PEs.h, PEs.n)
            ldw_for[hp + 2] = load_pair_w(hp + 2)
        elif hp == NPAIR - 2:
            sy.wait_ge(PEs.h, PEs.n)
            load_woutT(0, woutT_sb)
        if hp == 0:
            emit_compaction()
        pe.wait_ge(ACs.h, qkv_ac)

        attn_ready = {}
        trs_done = {}
        sums_of = {}

        def aux_of(i, base, h):
            sl = (i % 2) * 2 + h
            return s_aux[:, base + sl:base + sl + 1]

        def emit_norm(i):
            dv.wait_ge(ACs.h, sums_of[i])
            i_rc = None
            for h in (0, 1):
                i_rc = dv.reciprocal(aux_of(i, 20, h), aux_of(i, 16, h))
            rc = DVs.inc(i_rc)
            ac.wait_ge(DVs.h, rc)
            par = i % 2
            if par in abf_slot_prev:
                ac.wait_ge(PEs.h, abf_slot_prev[par])
            for h in (0, 1):
                po = (par * 2 + h) * 640
                i_n = ac.activation(out=a_bf[:, po:po + 640],
                                    in_=p_f32[:, po:po + 640],
                                    func=AF.Identity, scale=aux_of(i, 20, h))
                attn_ready[(i, h)] = (ACs, ACs.inc(i_n))

        def emit_scores(i, prev):
            par = i % 2
            sc = []
            for (h, bS, row0) in ((0, pbS0, 0), (1, pbS1, 64)):
                pe_guard(f"pbS{h}a"); pe_guard(f"pbS{h}b")
                lhs = q_sb[row0:row0 + 64, i * 128:(i + 1) * 128]
                rhs0 = k_pad[row0:row0 + 64, i * 128: i * 128 + 512]
                rhs1 = k_pad[row0:row0 + 64, i * 128 + 512: i * 128 + 640]
                tp = (row0, 0)
                pe.matmul(bS[:, 0:512], lhs, rhs0, start=True, stop=False,
                          tile_position=tp)
                pe.matmul(bS[:, 512:640], lhs, rhs1, start=True, stop=False,
                          tile_position=tp)
                pe.matmul(bS[:, 0:512], ident[:], bandn[:, 0:512],
                          start=False, stop=True)
                last = pe.matmul(bS[:, 512:640], ident[:], bandn[:, 512:640],
                                 start=False, stop=True)
                sc.append(PEs.inc(last))
            rc = None
            if prev is not None:
                dv.wait_ge(ACs.h, sums_of[prev])
                i_rc = None
                for h in (0, 1):
                    i_rc = dv.reciprocal(aux_of(prev, 20, h), aux_of(prev, 16, h))
                rc = DVs.inc(i_rc)
                ppar = prev % 2
                ac.wait_ge(DVs.h, rc)
                if ppar in abf_slot_prev:
                    ac.wait_ge(PEs.h, abf_slot_prev[ppar])
                po = ppar * 2 * 640
                i_n = ac.activation(out=a_bf[:, po:po + 640],
                                    in_=p_f32[:, po:po + 640],
                                    func=AF.Identity, scale=aux_of(prev, 20, 0))
                attn_ready[(prev, 0)] = (ACs, ACs.inc(i_n))
            dv.wait_ge(PEs.h, sc[0])
            i_m = dv.reduce_max(aux_of(i, 12, 0), pbS0[:, 0:640], axis=AX.X,
                                negate=True)
            m0 = DVs.inc(i_m)
            ac.wait_ge(DVs.h, m0)
            po = (par * 2) * 640
            i_e = ac.activation(out=p_f32[:, po:po + 640], in_=pbS0[:, 0:640],
                                func=AF.Exp, bias=aux_of(i, 12, 0),
                                accum_out=aux_of(i, 16, 0))
            e0 = ACs.inc(i_e)
            bank_free["pbS0a"] = (ACs, e0)
            bank_free["pbS0b"] = (ACs, e0)
            dv.wait_ge(PEs.h, sc[1])
            i_m = dv.reduce_max(aux_of(i, 12, 1), pbS1[:, 0:640], axis=AX.X,
                                negate=True)
            m1 = DVs.inc(i_m)
            if prev is not None:
                ppar = prev % 2
                dv.wait_ge(DVs.h, rc)
                if ppar in abf_slot_prev:
                    dv.wait_ge(PEs.h, abf_slot_prev[ppar])
                po = (ppar * 2 + 1) * 640
                i_n = dv.tensor_scalar(out=a_bf[:, po:po + 640],
                                       in0=p_f32[:, po:po + 640],
                                       scalar1=aux_of(prev, 20, 1), scalar2=None,
                                       op0=ALU.mult)
                attn_ready[(prev, 1)] = (DVs, DVs.inc(i_n))
            ac.wait_ge(DVs.h, m1)
            po = (par * 2 + 1) * 640
            i_e = ac.activation(out=p_f32[:, po:po + 640], in_=pbS1[:, 0:640],
                                func=AF.Exp, bias=aux_of(i, 12, 1),
                                accum_out=aux_of(i, 16, 1))
            sums_of[i] = ACs.inc(i_e)
            bank_free["pbS1a"] = (ACs, sums_of[i])
            bank_free["pbS1b"] = (ACs, sums_of[i])

        def emit_transposes(i):
            par = i % 2
            r = i % 4
            for (h, aT) in ((0, attnT0), (1, attnT1)):
                po = (par * 2 + h) * 640
                bname = f"pT{h}"
                bankT = pT0 if h == 0 else pT1
                pe_guard(bname)
                rs, rv = attn_ready[(i, h)]
                pe.wait_ge(rs.h, rv)
                last = None
                for c in range(5):
                    last = pe.transpose(bankT[:, c * 128:(c + 1) * 128],
                                        a_bf[:, po + c * 128:po + (c + 1) * 128],
                                        ident[:])
                pv = PEs.inc(last)
                dv.wait_ge(PEs.h, pv)
                dst = bass.AP(aT, r * 640, [[8 * 512, 128], [512, 5], [1, 128]])
                i_ev = dv.tensor_copy(
                    dst, bankT[:, 0:640].rearrange("p (c q) -> p c q", q=128))
                ev = DVs.inc(i_ev)
                bank_free[bname] = (DVs, ev)
            trs_done[i] = PEs.n
            abf_slot_prev[par] = PEs.n
            return DVs.n

        def emit_av(qb, trs_dv):
            pe_guard("pb0")
            pe_guard("pb1")
            pe.wait_ge(DVs.h, trs_dv)
            last = None
            d_order = [4, 0, 1, 2, 3, 5, 6, 7]
            for d in d_order:
                j = 4 * qb + d
                q0, q1 = max(0, d - 4) * 128, (min(3, d) + 1) * 128
                for (h, aT, bk, cp) in ((0, attnT0, pb0, 0), (1, attnT1, pb1, 64)):
                    last = pe.matmul(bk[cp:cp + 64, q0:q1],
                                     v_tm[:, j * 128 + cp: j * 128 + cp + 64],
                                     aT[:, d * 512 + q0: d * 512 + q1],
                                     start=(d == 4), stop=(d == 7),
                                     tile_position=(0, cp))
            av_pe = PEs.inc(last)
            ac.wait_ge(PEs.h, av_pe)
            ocol = hp * T + qb * 512
            ac.copy(out=o_all[0:64, ocol:ocol + 512], in_=pb0[0:64, :])
            i_ev = ac.copy(out=o_all[64:128, ocol:ocol + 512], in_=pb1[64:128, :])
            av_ac = ACs.inc(i_ev)
            bank_free["pb0"] = (ACs, av_ac)
            bank_free["pb1"] = (ACs, av_ac)
            return av_pe

        order = [3, 0, 1, 2]
        seq = [qb * 4 + r for qb in range(4) for r in order]
        tdv = 0
        for k in range(len(seq)):
            if k >= 2:
                tdv = emit_transposes(seq[k - 2])
            emit_scores(seq[k], seq[k - 1] if k >= 1 else None)
            if k >= 2 and (k - 2) % 4 == 3:
                last_av_pe = attn_done_pe = emit_av((k - 2) // 4, tdv)
        emit_norm(seq[15])
        tdv = emit_transposes(seq[14])
        tdv = emit_transposes(seq[15])
        last_av_pe = attn_done_pe = emit_av(3, tdv)
        if hp == 0:
            route2_dv = emit_route2()
            for e in range(5):
                gp.wait_ge(CP.h, CP.n)
                CP.inc(gp.dma_start(out=w1b_d[e, :, :], in_=w1_d[e, :, :]), 16)
                gp.wait_ge(CP.h, CP.n)
                CP.inc(gp.dma_start(out=w2b_d[e, :, :], in_=w2_d[e, :, :]), 16)
                cast_done[e] = CP.n

    gp.wait_ge(DVs.h, route2_dv)
    for k in range(4):
        gp.wait_ge(XNB4[k].h, XNB4[k].n)
    gp.wait_ge(PEs.h, attn_done_pe)
    for g in range(5):
        GX.inc(gp.indirect_dma_start(
            out=xe_tm[:, g * 1024:(g + 1) * 1024],
            out_offset=None,
            in_=xnb_d.ap(),
            in_offset=IndirectOffsetOnAxis(toks_u32[:, g:g + 1], 0)), 16)
    gather0_done = GX.n

    w1buf_free = [(PEs, attn_done_pe), (PEs, attn_done_pe)]
    w1_loads = {}
    w2_loads = {}

    def load_w1(e, fg, buf):
        s, v = w1buf_free[buf]
        dstw = w1_sb[:, buf * KC * 512:(buf + 1) * KC * 512]
        d3 = dstw.rearrange("p (c f) -> p c f", f=512)
        if e in cast_done:
            sy.wait_ge(s.h, v)
            sy.wait_ge(CP.h, cast_done[e])
            for hf in range(2):
                srcw = w1b_d[e, hf * 512:(hf + 1) * 512, fg * 512:(fg + 1) * 512] \
                    .rearrange("(c p) f -> p c f", p=128)
                v = W1S[buf].inc(sy.dma_start(out=d3[:, hf * 4:(hf + 1) * 4, :],
                                              in_=srcw), 16)
            w1_loads[(e, fg)] = (W1S[buf], buf, v)
        else:
            gp.wait_ge(s.h, v)
            for hf in range(2):
                srcw = w1_d[e, hf * 512:(hf + 1) * 512, fg * 512:(fg + 1) * 512] \
                    .rearrange("(c p) f -> p c f", p=128)
                v = W1G[buf].inc(gp.dma_start(out=d3[:, hf * 4:(hf + 1) * 4, :],
                                              in_=srcw), 16)
            w1_loads[(e, fg)] = (W1G[buf], buf, v)

    def load_w2(e, dhl, fg, buf):
        s, v = w2buf_free[buf]
        dstw = w2_sb[:, buf * 4 * 512:(buf + 1) * 4 * 512]
        d3 = dstw.rearrange("p (c o) -> p c o", o=512)
        if e in cast_done:
            sy.wait_ge(s.h, v)
            sy.wait_ge(CP.h, cast_done[e])
            for hf in range(2):
                srcw = w2b_d[e, fg * 512 + hf * 256:fg * 512 + (hf + 1) * 256,
                             dhl * 512:(dhl + 1) * 512] \
                    .rearrange("(c p) o -> p c o", p=128)
                v = W2S[buf].inc(sy.dma_start(out=d3[:, hf * 2:(hf + 1) * 2, :],
                                              in_=srcw), 16)
            w2_loads[(e, dhl, fg)] = (W2S[buf], buf, v)
        else:
            gp.wait_ge(s.h, v)
            for hf in range(2):
                srcw = w2_d[e, fg * 512 + hf * 256:fg * 512 + (hf + 1) * 256,
                            dhl * 512:(dhl + 1) * 512] \
                    .rearrange("(c p) o -> p c o", p=128)
                v = W2G[buf].inc(gp.dma_start(out=d3[:, hf * 2:(hf + 1) * 2, :],
                                              in_=srcw), 16)
            w2_loads[(e, dhl, fg)] = (W2G[buf], buf, v)

    load_w1(0, 0, 0)
    sy.wait_ge(PEs.h, attn_done_pe)
    load_woutT(1, woutT1_sb)

    st_done = {}
    oproj_ac = ACs.n
    op_banks = [("pb0", pb0[:, 0:512]), ("pb1", pb1[:, 0:512]),
                ("pbS0a", pbS0[:, 0:512]), ("pbS1a", pbS1[:, 0:512])]
    for dhl in range(2):
        pe.wait_ge(LDO.h, LDO.n)
        pe.wait_ge(LDW.h, LDW.n)
        if dhl == 0:
            pe.wait_ge(ACs.h, oproj_ac)
        wsrc = woutT_sb if dhl == 0 else woutT1_sb
        for i in range(NT):
            bname, bank = op_banks[i % 4]
            pe_guard(bname)
            if bname in ("pbS0a", "pbS1a"):
                pe_guard(bname[:-1] + "b")
            for c in range(KC):
                pe.matmul(bank, o_all[:, c * T + i * 128:c * T + (i + 1) * 128],
                          wsrc[:, c * 512:(c + 1) * 512],
                          start=(c == 0), stop=False)
            last = pe.matmul(bank, ones_r[0:1, :],
                             bout_r[0:1, dhl * 512:(dhl + 1) * 512],
                             start=False, stop=True)
            pv = PEs.inc(last)
            dv.wait_ge(PEs.h, pv)
            ysl = y_sb[:, i * D + dhl * 512: i * D + (dhl + 1) * 512]
            i_ev = dv.scalar_tensor_tensor(out=ysl, in0=bank, scalar=1.0,
                                           in1=ysl, op0=ALU.mult, op1=ALU.add)
            ev = DVs.inc(i_ev)
            bank_free[bname] = (DVs, ev)
            if bname in ("pbS0a", "pbS1a"):
                bank_free[bname[:-1] + "b"] = (DVs, ev)
        sy.wait_ge(DVs.h, DVs.n)
        yh_d = y_d[0:T, dhl * 512:(dhl + 1) * 512].rearrange("(i p) d -> p i d", p=128)
        yh_s = y_sb.ap().rearrange("p (i d) -> p i d", d=D)[:, :, dhl * 512:(dhl + 1) * 512]
        STH[dhl].inc(sy.dma_start(out=yh_d[:, 0:8, :], in_=yh_s[:, 0:8, :]), 16)
        STH[dhl].inc(sy.dma_start(out=yh_d[:, 8:16, :], in_=yh_s[:, 8:16, :]), 16)
        st_done[dhl] = STH[dhl].n
        if dhl == 0:
            load_w1(0, 1, 1)

    for g in range(5):
        bnameT = f"pT{g % 2}"
        bankT = pT0 if g % 2 == 0 else pT1
        pe_guard(bnameT)
        pe.wait_ge(GX.h, gather0_done)
        last = None
        for c in range(KC):
            last = pe.transpose(bankT[:, c * 128:(c + 1) * 128],
                                xe_tm[:, g * 1024 + c * 128:g * 1024 + (c + 1) * 128],
                                ident[:])
        pv = PEs.inc(last)
        dv.wait_ge(PEs.h, pv)
        dstx = bass.AP(xe_fm, g * 128, [[KC * 640, 128], [640, KC], [1, 128]])
        i_ev = dv.tensor_copy(dstx, bankT.ap().rearrange("p (c q) -> p c q", q=128))
        ev = DVs.inc(i_ev)
        bank_free[bnameT] = (DVs, ev)
    xe0_trs_pe = PEs.n
    xe0_fm_dv = DVs.n

    if debug:
        attnres_dv = DVs.n
        dbg_oa = nc.dram_tensor("dbg_oa", [128, NPAIR * T], F32, kind="ExternalOutput")
        sy.wait_ge(DVs.h, attnres_dv)
        gp.wait_ge(DVs.h, attnres_dv)
        LD.inc(gp.dma_start(out=dbg_oa[:, :], in_=o_all[:]), 16)
    if debug == 2:
        ST.inc(sy.dma_start(out=y_d[0:T, :].rearrange("(i p) d -> p i d", p=128),
                            in_=y_sb.ap().rearrange("p (i d) -> p i d", d=D)), 16)
        sy.wait_ge(ST.h, ST.n)
        sy.wait_ge(LX.h, LX.n)
        gp.wait_ge(LD.h, LD.n)
        gp.wait_ge(LDW.h, LDW.n)
        gp.wait_ge(LDO.h, LDO.n)
        es.close()
        return nc

    oproj_end_pe = PEs.n
    w2buf_free = [(PEs, oproj_end_pe), (PEs, oproj_end_pe)]

    w1_order = [(e, fg) for e in range(NE) for fg in range(8)]
    w2_order = [(e, dhl, fg) for e in range(NE)
                for dhl in range(2) for fg in range(8)]
    w1_next = 2
    w2_next = 2
    load_w2(*w2_order[0], 0)
    load_w2(*w2_order[1], 1)

    gather_done = {}
    xe_trs_done = {}

    def gather_xe(e):
        dst = xe_tm
        gp.wait_ge(DVs.h, route2_dv)
        for k in range(4):
            gp.wait_ge(XNB4[k].h, XNB4[k].n)
        gp.wait_ge(PEs.h, xe_trs_done[e - 1])
        for g in range(NG):
            GX.inc(gp.indirect_dma_start(
                out=dst[:, g * 1024:(g + 1) * 1024],
                out_offset=None,
                in_=xnb_d.ap(),
                in_offset=IndirectOffsetOnAxis(
                    toks_u32[:, e * NG + g:e * NG + g + 1], 0)), 16)
        gather_done[e] = GX.n

    m1_banks = [("pbS0a", pbS0[:, 0:512]), ("pbS1a", pbS1[:, 0:512]),
                ("pbS0b", pbS0[:, 512:1024]), ("pbS1b", pbS1[:, 512:1024])]
    m2_banks = [("pb0", pb0[:, 0:512]), ("pb1", pb1[:, 0:512]),
                ("pbS0a", pbS0[:, 0:512]), ("pbS0b", pbS0[:, 512:1024]),
                ("pbS1a", pbS1[:, 0:512])]
    sct_expert_end = {}
    sct_half_prev = 0
    sct_hist = []

    def emit_xe_trans(e):
        srcb = xe_tm
        pe.wait_ge(GX.h, gather_done[e])
        for g in range(NG):
            bname = f"pT{g % 2}"
            bankT = pT0 if g % 2 == 0 else pT1
            pe_guard(bname)
            last = None
            for c in range(KC):
                last = pe.transpose(bankT[:, c * 128:(c + 1) * 128],
                                    srcb[:, g * 1024 + c * 128:g * 1024 + (c + 1) * 128],
                                    ident[:])
            pv = PEs.inc(last)
            dv.wait_ge(PEs.h, pv)
            dstx = bass.AP(xe_fm, g * 128, [[KC * 640, 128], [640, KC], [1, 128]])
            i_ev = dv.tensor_copy(dstx, bankT.ap().rearrange("p (c q) -> p c q", q=128))
            ev = DVs.inc(i_ev)
            bank_free[bname] = (DVs, ev)
        xe_trs_done[e] = PEs.n
        return DVs.n

    gather_done[0] = gather0_done
    xe_trs_done[0] = xe0_trs_pe
    xe_fm_ready = xe0_fm_dv
    gather_xe(1)

    for e in range(NE):
        first_m1 = True
        for fg in range(8):
            lsem, buf, ldv = w1_loads[(e, fg)]
            pe.wait_ge(lsem.h, ldv)
            if first_m1:
                pe.wait_ge(DVs.h, xe_fm_ready)
                first_m1 = False
            w1base = buf * KC * 512
            for fc in range(4):
                for blk, (b0, bw) in enumerate(((0, 512), (512, 128))):
                    bname, bank = m1_banks[(fc * 2 + blk) % 4]
                    pe_guard(bname)
                    last = None
                    for c in range(KC):
                        last = pe.matmul(
                            bank[:, 0:bw],
                            w1_sb[:, w1base + c * 512 + fc * 128:
                                  w1base + c * 512 + (fc + 1) * 128],
                            xe_fm[:, c * 640 + b0: c * 640 + b0 + bw],
                            start=(c == 0), stop=(c == KC - 1))
                    pv = PEs.inc(last)
                    ac.wait_ge(PEs.h, pv)
                    fi = fg * 4 + fc
                    i_ev = ac.activation(
                        out=h_sb[:, fi * 640 + b0: fi * 640 + b0 + bw],
                        in_=bank[:, 0:bw], func=AF.Relu)
                    av = ACs.inc(i_ev)
                    bank_free[bname] = (ACs, av)
            w1buf_free[buf] = (PEs, PEs.n)
            if w1_next < len(w1_order):
                load_w1(*w1_order[w1_next], w1_next % 2)
                w1_next += 1
        m1_ac = ACs.n
        if e + 1 < NE:
            xe_fm_ready = emit_xe_trans(e + 1)
            if e + 2 < NE:
                gather_xe(e + 2)
        first_m2 = True
        for dhl in range(2):
            for fg in range(8):
                lsem, buf, ldv = w2_loads[(e, dhl, fg)]
                pe.wait_ge(lsem.h, ldv)
                if first_m2:
                    pe.wait_ge(ACs.h, m1_ac)
                    first_m2 = False
                w2base = buf * 4 * 512
                last = None
                for gi_ in range(NG):
                    bname, bank = m2_banks[gi_]
                    if fg == 0:
                        pe_guard(bname)
                    for fc in range(4):
                        fi = fg * 4 + fc
                        last = pe.matmul(
                            bank,
                            h_sb[:, fi * 640 + gi_ * 128: fi * 640 + (gi_ + 1) * 128],
                            w2_sb[:, w2base + fc * 512:w2base + (fc + 1) * 512],
                            start=(fg == 0 and fc == 0),
                            stop=(fg == 7 and fc == 3))
                w2buf_free[buf] = (PEs, PEs.inc(last))
                if w2_next < len(w2_order):
                    load_w2(*w2_order[w2_next], w2_next % 2)
                    w2_next += 1
            m2_pe = PEs.n
            dv.wait_ge(PEs.h, m2_pe)
            dv.wait_ge(SCT.h, sct_half_prev)
            for gi_ in range(NG):
                bname, bank = m2_banks[gi_]
                slot = gi_
                i_ev = dv.tensor_scalar(
                    out=ysc[:, slot * 512:(slot + 1) * 512], in0=bank,
                    scalar1=gate_sl[:, e * NG + gi_:e * NG + gi_ + 1],
                    scalar2=None, op0=ALU.mult)
                ev = DVs.inc(i_ev)
                bank_free[bname] = (DVs, ev)
                gp.wait_ge(DVs.h, ev)
                if e > 0:
                    gp.wait_ge(SCT.h, sct_expert_end[e - 1])
                else:
                    gp.wait_ge(STH[dhl].h, st_done[dhl])
                i_sc = gp.indirect_dma_start(
                    out=y_d.ap(),
                    out_offset=IndirectOffsetOnAxis(
                        toks_u32[:, e * NG + gi_:e * NG + gi_ + 1], 0),
                    in_=ysc[:, slot * 512:(slot + 1) * 512],
                    in_offset=None,
                    element_offset=dhl * 512,
                    compute_op=ALU.add)
                sct_hist.append(SCT.inc(i_sc, 16))
            sct_half_prev = SCT.n
        sct_expert_end[e] = SCT.n

    gp.wait_ge(SCT.h, SCT.n)
    for b in range(2):
        gp.wait_ge(W1G[b].h, W1G[b].n)
        gp.wait_ge(W2G[b].h, W2G[b].n)
    sy.wait_ge(SCT.h, SCT.n)
    sy.wait_ge(STH[0].h, STH[0].n)
    sy.wait_ge(STH[1].h, STH[1].n)
    sy.wait_ge(ST.h, ST.n)
    sy.wait_ge(LX.h, LX.n)
    sy.wait_ge(LD.h, LD.n)
    gp.wait_ge(LDW.h, LDW.n)
    gp.wait_ge(LDP.h, LDP.n)
    sy.wait_ge(LDO.h, LDO.n)
    gp.wait_ge(CP.h, CP.n)
    gp.wait_ge(SCI.h, SCI.n)
    gp.wait_ge(SCR.h, SCR.n)
    gp.wait_ge(GX.h, GX.n)
    for k in range(4):
        sy.wait_ge(XNB4[k].h, XNB4[k].n)
    for s in (W1S[0], W1S[1], W2S[0], W2S[1]):
        sy.wait_ge(s.h, s.n)
    es.close()
    return nc


_NC_CACHE = None


def kernel(L0_attn_x, L0_attn_Wqkv, L0_attn_bqkv, L0_attn_Wout, L0_attn_bout,
           L0_ffn_Wr, L0_ffn_W1, L0_ffn_W2):
    global _NC_CACHE
    x = np.ascontiguousarray(np.asarray(L0_attn_x, dtype=np.float32))
    wqkvT = np.ascontiguousarray(np.asarray(L0_attn_Wqkv, dtype=np.float32).T)
    bqkv = np.ascontiguousarray(np.asarray(L0_attn_bqkv, dtype=np.float32))
    woutT = np.ascontiguousarray(np.asarray(L0_attn_Wout, dtype=np.float32).T)
    bout = np.ascontiguousarray(np.asarray(L0_attn_bout, dtype=np.float32))
    wr = np.ascontiguousarray(np.asarray(L0_ffn_Wr, dtype=np.float32))
    w1 = np.ascontiguousarray(np.asarray(L0_ffn_W1, dtype=np.float32))
    w2 = np.ascontiguousarray(np.asarray(L0_ffn_W2, dtype=np.float32))

    if _NC_CACHE is None:
        _NC_CACHE = build_nc()
    nc = _NC_CACHE

    ident_np = np.eye(128, dtype=ml_dtypes.bfloat16)
    ones_np = np.ones((1, 128), dtype=ml_dtypes.bfloat16)
    iota8_np = np.tile(np.arange(8, dtype=np.float32), (128, 1))
    bandn_np = np.full((128, 640), -30000.0, dtype=np.float32)
    for r in range(128):
        bandn_np[r, r:r + WIN] = 0.0
    bandn_np = bandn_np.astype(ml_dtypes.bfloat16)
    tri_np = np.zeros((128, 128), dtype=ml_dtypes.bfloat16)
    for r in range(128):
        tri_np[r, r + 1:] = 1.0
    ones128_np = np.ones((128, 128), dtype=ml_dtypes.bfloat16)
    tokid_np = (np.arange(NT, dtype=np.float32)[None, :] * 128
                + np.arange(128, dtype=np.float32)[:, None])

    in_maps = []
    for b in range(8):
        in_maps.append({
            "x": x[b], "wqkvT": wqkvT, "bqkv": bqkv, "woutT": woutT,
            "bout": bout, "wr": wr, "w1": w1, "w2": w2,
            "ident_c": ident_np, "ones_c": ones_np, "iota8_c": iota8_np,
            "bandn_c": bandn_np, "tri_c": tri_np, "ones128_c": ones128_np,
            "tokid_c": tokid_np, "ident32_c": np.eye(128, dtype=np.float32),
        })
    res = run_bass_kernel_spmd(nc, in_maps, core_ids=list(range(8)))
    out = np.stack([res.results[b]["y"][:T] for b in range(8)], axis=0)
    return out
